# version 01
# baseline (speedup 1.0000x reference)
import contextlib
import numpy as np
import concourse.bass as bass
import concourse.mybir as mybir

F32 = mybir.dt.float32
BF16 = mybir.dt.bfloat16
AF = mybir.ActivationFunctionType
ALU = mybir.AluOpType
AX = mybir.AxisListType

D = 1024
T = 2048
NCH = 8


class Ctx:
    def __init__(self, nc, stack):
        self.nc = nc
        self.stack = stack
        self.uid = 0
        self.consts = {}
        self.dram = {}

    def name(self, base):
        self.uid += 1
        return f"{base}_{self.uid}"

    def sem(self, stack=None):
        return self.sems(1, stack)[0]

    def sems(self, n, stack=None):
        st = stack or self.stack
        hs = [st.enter_context(self.nc.semaphore(self.name("sem"))) for _ in range(n)]
        with self.nc.Block() as b:
            def _clear(v, hs=hs):
                for h in hs:
                    v.sem_clear(h)
            b.vector(_clear)
        return hs

    def dsem(self, stack=None, kind="sw"):
        if not hasattr(self, "_dpool"):
            self._dpool = {"sw": [], "hw": []}
            self._dfree = {"sw": [], "hw": []}
            self._dtot = {"sw": {}, "hw": {}}
        if self._dfree[kind]:
            idx = self._dfree[kind].pop()
        else:
            h = self.stack.enter_context(self.nc.semaphore(self.name("d" + kind)))
            idx = len(self._dpool[kind])
            self._dpool[kind].append(h)
            self._dtot[kind][idx] = 0
        d = DSem(self, kind, idx)
        (stack or self.stack).callback(lambda k=kind, i=idx: self._dfree[k].append(i))
        return d

    def sbuf(self, shape, dtype, stack=None, base=None):
        return (stack or self.stack).enter_context(
            self.nc.sbuf_tensor(self.name(base or "sb"), shape, dtype))

    def psum(self, shape, dtype=F32, stack=None, base=None):
        return (stack or self.stack).enter_context(
            self.nc.psum_tensor(self.name(base or "ps"), shape, dtype))

    def barrier_all(self, sem, val):
        nc = self.nc
        with nc.Block() as b:
            for engname in ("vector", "tensor", "scalar", "gpsimd", "sync"):
                getattr(b, engname)(lambda e: e.wait_ge(sem, val))

    def esync(self, stack=None):
        return ESync(self, stack)

    def register_fconsts(self, vals):
        nc = self.nc
        vals = [v for v in vals if v not in (0.0, 1.0)]
        tiles = [(v, self.sbuf([128, 1], F32, base="fc")) for v in vals]
        with nc.Block() as b:
            @b.vector
            def _(v):
                for val, tile in tiles:
                    v.memset(tile[:], float(val))
        for val, tile in tiles:
            nc.const_aps.register(F32, float(val), tile[:, 0:1])


class DSem:

    def __init__(self, ctx, kind, idx):
        self.ctx, self.kind, self.idx = ctx, kind, idx
        self.sem = ctx._dpool[kind][idx]
        self.base = ctx._dtot[kind][idx]

    def inc(self, instr, n=16):
        self.ctx._dtot[self.kind][self.idx] += n
        instr.then_inc(self.sem, n)

    def rel(self):
        return self.ctx._dtot[self.kind][self.idx] - self.base

    def wait(self, eng, rel):
        eng.wait_ge(self.sem, self.base + rel)

    def barrier(self, rel):
        self.ctx.barrier_all(self.sem, self.base + rel)


class SlotSems:

    def __init__(self, ctx, stack, nslots, kind="sw"):
        self.ds = [ctx.dsem(stack, kind) for _ in range(nslots)]
        self.sems = [d.sem for d in self.ds]

    def inc(self, dma_instr, slot):
        self.ds[slot].inc(dma_instr, 16)

    def wait(self, eng, slot, rel):
        self.ds[slot].wait(eng, rel)

    def barrier(self, slot, rel):
        self.ds[slot].barrier(rel)


class ESync:

    def __init__(self, ctx, stack=None):
        self.sem = ctx.sem(stack)
        self.n = 0

    def edge(self, eng, instr=None):
        self.n += 1
        if instr is not None:
            instr.then_inc(self.sem, 1)
        else:
            eng.sem_inc(self.sem, 1)
        eng.wait_ge(self.sem, self.n)



def fm_layout(x_td):
    Tn, Dn = x_td.shape
    nch = Dn // 128
    return np.ascontiguousarray(x_td.T.reshape(nch, 128, Tn).transpose(1, 0, 2))


def fm_unlayout(x_fm):
    p, nch, Tn = x_fm.shape
    return np.ascontiguousarray(x_fm.transpose(1, 0, 2).reshape(nch * 128, Tn).T)


def col_layout(vec):
    n = vec.shape[0]
    return np.ascontiguousarray(vec.reshape(n // 128, 128).T)


def base_consts():
    return {
        "ones_col_bf": np.ones((128, 1), np.float32),
        "ones_col_f32": np.ones((128, 1), np.float32),
        "ones_row_f32": np.ones((1, 128), np.float32),
        "ones_row_bf": np.ones((1, 128), np.float32),
        "zeros_row512": np.zeros((1, 512), np.float32),
        "ones_row512": np.ones((1, 512), np.float32),
        "negones_row_f32": -np.ones((1, 128), np.float32),
        "ones_sq": np.ones((128, 128), np.float32),
        "ident128_bf": np.eye(128, dtype=np.float32),
    }


CONST_DTYPES = {"ones_col_bf": BF16, "ones_row_bf": BF16, "zeros_row512": BF16, "ones_row512": BF16, "ident128_bf": BF16}


def declare_inputs(ctx, arrays):
    nc = ctx.nc
    for name, arr in arrays.items():
        ctx.dram[name] = nc.dram_tensor(name, list(arr.shape), F32,
                                        kind="ExternalInput").ap()
    return ctx.dram


def load_consts(ctx, names, dtypes=None):
    nc = ctx.nc
    dtypes = dict(CONST_DTYPES, **(dtypes or {}))
    sem_hw = ctx.dsem(kind="hw")
    sem_sw = ctx.dsem(kind="sw")
    thw = [0]
    tsw = [0]
    with nc.Block() as b:
        @b.sync
        def _(sync):
            for name in names:
                if dtypes.get(name, F32) == F32:
                    dr = ctx.dram[name]
                    sb = ctx.sbuf(list(dr.shape), F32, base="c_" + name)
                    ctx.consts[name] = sb
                    sem_hw.inc(sync.dma_start(out=sb[:], in_=dr))
                    thw[0] += 16
        @b.gpsimd
        def _(g):
            for name in names:
                dt = dtypes.get(name, F32)
                if dt != F32:
                    dr = ctx.dram[name]
                    sb = ctx.sbuf(list(dr.shape), dt, base="c_" + name)
                    ctx.consts[name] = sb
                    sem_sw.inc(g.dma_start(out=sb[:], in_=dr))
                    tsw[0] += 16
    if thw[0]:
        sem_hw.barrier(thw[0])
    if tsw[0]:
        sem_sw.barrier(tsw[0])



def emit_glue_ln(ctx, X, XLN, keep_stats=False, eps=1e-5):
    nc = ctx.nc
    with contextlib.ExitStack() as stack:
        sstat = ctx.stack if keep_stats else stack
        mu = ctx.sbuf([1, T], F32, stack=sstat, base="gl_mu")
        rstd = ctx.sbuf([1, T], F32, stack=sstat, base="gl_rstd")
        x2db = ctx.sbuf([128, NCH, 2, 512], BF16, stack=stack, base="gl_x2db")
        ms = ctx.sbuf([1, T], F32, stack=stack, base="gl_ms")
        tmps = ctx.sbuf([128, NCH, 512], F32, stack=stack, base="gl_tmp")
        sem_cv = ctx.sem(stack)
        sem_cs = ctx.sem(stack)
        sem_mm = ctx.sem(stack)
        with contextlib.ExitStack() as pstack:
            ps_sx = ctx.psum([1, T], F32, stack=pstack, base="gl_ps")
            ps_sx2 = ctx.psum([1, T], F32, stack=pstack, base="gl_ps2")
            with nc.Block() as b:
                @b.vector
                def _(v):
                    for blk in range(4):
                        f0 = blk * 512
                        for c in range(NCH):
                            i = v.tensor_copy(out=XLN[:, c, f0:f0 + 512], in_=X[:, c, f0:f0 + 512])
                        i.then_inc(sem_cv, 1)
                @b.scalar
                def _(s):
                    for blk in range(4):
                        f0 = blk * 512
                        if blk >= 2:
                            s.wait_ge(sem_mm, blk - 1)
                        for c in range(NCH):
                            i = s.activation(out=x2db[:, c, blk % 2, :], in_=X[:, c, f0:f0 + 512],
                                             func=AF.Square)
                        i.then_inc(sem_cs, 1)
                @b.tensor
                def _(t):
                    oc = ctx.consts["ones_col_bf"]
                    for blk in range(4):
                        f0 = blk * 512
                        t.wait_ge(sem_cv, blk + 1)
                        t.wait_ge(sem_cs, blk + 1)
                        for c in range(NCH):
                            t.matmul(ps_sx[0:1, f0:f0 + 512], oc[:, 0:1],
                                     XLN[:, c, f0:f0 + 512],
                                     start=(c == 0), stop=(c == NCH - 1))
                        for c in range(NCH):
                            mm = t.matmul(ps_sx2[0:1, f0:f0 + 512], oc[:, 0:1],
                                          x2db[:, c, blk % 2, :],
                                          start=(c == 0), stop=(c == NCH - 1))
                        mm.then_inc(sem_mm, 1)
            with nc.Block() as b:
                @b.vector
                def _(v):
                    v.tensor_scalar_mul(out=mu[:], in0=ps_sx[0:1, :], scalar1=1.0 / D)
                    v.tensor_scalar_mul(out=ms[:], in0=ps_sx2[0:1, :], scalar1=1.0 / D)
        es = ESync(ctx, stack)
        with nc.Block() as b:
            @b.vector
            def _(v):
                i = v.tensor_mul(out=rstd[:], in0=mu[:], in1=mu[:])
                es.edge(v, i)
                v.tensor_sub(out=ms[:], in0=ms[:], in1=rstd[:])
        with nc.Block() as b:
            @b.scalar
            def _(s):
                i = s.activation(out=ms[:], in_=ms[:], func=AF.Ln, bias=float(eps))
                es.edge(s, i)
                s.activation(out=rstd[:], in_=ms[:], func=AF.Exp, scale=-0.5)
        with contextlib.ExitStack() as pstack:
            ps_mu = ctx.psum([128, T], F32, stack=pstack, base="gl_bmu")
            ps_r = ctx.psum([128, T], F32, stack=pstack, base="gl_br")
            with nc.Block() as b:
                @b.tensor
                def _(t):
                    orow = ctx.consts["ones_row_f32"]
                    for blk in range(4):
                        f0 = blk * 512
                        t.matmul(ps_mu[:, f0:f0 + 512], orow[0:1, :],
                                 mu[0:1, f0:f0 + 512], start=True, stop=True)
                        t.matmul(ps_r[:, f0:f0 + 512], orow[0:1, :],
                                 rstd[0:1, f0:f0 + 512], start=True, stop=True)
            with nc.Block() as b:
                @b.vector
                def _(v):
                    for blk in range(4):
                        f0 = blk * 512
                        for c in range(NCH):
                            i = v.tensor_sub(out=tmps[:, c, :], in0=X[:, c, f0:f0 + 512],
                                             in1=ps_mu[:, f0:f0 + 512])
                        es.edge(v, i)
                        for c in range(NCH):
                            i = v.tensor_mul(out=XLN[:, c, f0:f0 + 512], in0=tmps[:, c, :],
                                             in1=ps_r[:, f0:f0 + 512])
                        es.edge(v, i)
    return (mu, rstd) if keep_stats else None



def emit_l3_ffn_prep(ctx, scr_w1t, scr_w2t, dbg=None):
    nc = ctx.nc
    for (wname, scr, K, M, scale) in (("l3_w1T", scr_w1t, 1024, 4096, 1.0),
                                      ("l3_w2T", scr_w2t, 4096, 1024, 0.5)):
        W = ctx.dram[wname]
        with contextlib.ExitStack() as stack:
            kch = K // 128
            wbuf = ctx.sbuf([128, 2, M], F32, stack=stack, base="l3p_w")
            acc = ctx.sbuf([128, 1], F32, stack=stack, base="l3p_acc")
            part = ctx.sbuf([128, kch], F32, stack=stack, base="l3p_part")
            hs = ctx.sbuf([128, 2], F32, stack=stack, base="l3p_hs")
            hrow = ctx.sbuf([1, 2], F32, stack=stack, base="l3p_hrow")
            ta = ctx.sbuf([128, M], F32, stack=stack, base="l3p_ta")
            tb = ctx.sbuf([128, M], F32, stack=stack, base="l3p_tb")
            tern = ctx.sbuf([128, 2, M], BF16, stack=stack, base="l3p_tern")
            ld = SlotSems(ctx, stack, 2, kind="hw")
            st = SlotSems(ctx, stack, 2, kind="sw")
            sem_v = ctx.sem(stack)
            es = ESync(ctx, stack)
            with nc.Block() as b:
                @b.sync
                def _(sy):
                    for k in range(kch):
                        if k >= 2:
                            sy.wait_ge(sem_v, k - 1)
                        ld.inc(sy.dma_start(out=wbuf[:, k % 2, :],
                                            in_=W[k * 128:(k + 1) * 128, :]), k % 2)
                @b.vector
                def _(v):
                    for k in range(kch):
                        ld.wait(v, k % 2, 16 * (k // 2 + 1))
                        v.tensor_reduce(out=part[:, k:k + 1], in_=wbuf[:, k % 2, :],
                                        axis=AX.X, op=ALU.add,
                                        apply_absolute_value=True).then_inc(sem_v, 1)
                    v.wait_ge(sem_v, kch)
                    v.tensor_reduce(out=acc[:], in_=part[:, 0:kch], axis=AX.X, op=ALU.add)
            with contextlib.ExitStack() as ps_stack:
                ps1 = ctx.psum([1, 2], F32, stack=ps_stack, base="l3p_ps1")
                ps2 = ctx.psum([128, 2], F32, stack=ps_stack, base="l3p_ps2")
                with nc.Block() as b:
                    @b.tensor
                    def _(t):
                        t.matmul(ps1[0:1, 0:1], acc[:, 0:1],
                                 ctx.consts["ones_col_f32"][:, 0:1], start=True, stop=True)
                with nc.Block() as b:
                    @b.vector
                    def _(v):
                        v.tensor_scalar(out=ps1[0:1, 0:1], in0=ps1[0:1, 0:1],
                                        scalar1=1.0 / (K * M), scalar2=1e-8,
                                        op0=ALU.mult, op1=ALU.max)
                with nc.Block() as b:
                    @b.vector
                    def _(v):
                        v.tensor_scalar_mul(out=hrow[0:1, 0:1], in0=ps1[0:1, 0:1], scalar1=0.5)
                        v.tensor_scalar_mul(out=hrow[0:1, 1:2], in0=ps1[0:1, 0:1], scalar1=-0.5)
                with nc.Block() as b:
                    @b.tensor
                    def _(t):
                        t.matmul(ps2[:, 0:2], ctx.consts["ones_row_f32"][0:1, :],
                                 hrow[0:1, 0:2], start=True, stop=True)
                with nc.Block() as b:
                    @b.vector
                    def _(v):
                        v.tensor_copy(out=hs[:], in_=ps2[:, 0:2])
                if dbg is not None:
                    with nc.Block() as b:
                        @b.vector
                        def _(v):
                            v.tensor_copy(out=dbg[0][:, (0 if wname == "l3_w1T" else 2):(2 if wname == "l3_w1T" else 4)], in_=hs[:])
            with nc.Block() as b:
                @b.sync
                def _(sy):
                    for k in range(kch):
                        if k >= 2:
                            sy.wait_ge(sem_v, kch + (k - 1))
                        ld.inc(sy.dma_start(out=wbuf[:, k % 2, :],
                                            in_=W[k * 128:(k + 1) * 128, :]), k % 2)
                @b.vector
                def _(v):
                    for k in range(kch):
                        ld.wait(v, k % 2, 16 * ((kch + k) // 2 + 1))
                        if k >= 2:
                            st.wait(v, k % 2, 16 * ((k - 2) // 2 + 1))
                        if k > 0:
                            v.wait_ge(sem_v, kch + k)
                        i = v.tensor_scalar(out=ta[:], in0=wbuf[:, k % 2, :],
                                            scalar1=hs[:, 0:1], scalar2=None, op0=ALU.is_gt)
                        i = v.tensor_scalar(out=tb[:], in0=wbuf[:, k % 2, :],
                                            scalar1=hs[:, 1:2], scalar2=None, op0=ALU.is_lt)
                        es.edge(v, i)
                        if scale == 1.0:
                            v.tensor_sub(out=tern[:, k % 2, :], in0=ta[:], in1=tb[:]).then_inc(sem_v, 1)
                        else:
                            i = v.tensor_sub(out=tb[:], in0=ta[:], in1=tb[:])
                            es.edge(v, i)
                            v.tensor_scalar_mul(out=tern[:, k % 2, :], in0=tb[:],
                                                scalar1=scale).then_inc(sem_v, 1)
                @b.gpsimd
                def _(g):
                    for k in range(kch):
                        g.wait_ge(sem_v, kch + k + 1)
                        st.inc(g.dma_start(out=scr[k * 128:(k + 1) * 128, :],
                                           in_=tern[:, k % 2, :]), k % 2)
            st.barrier(0, 16 * (kch // 2))
            st.barrier(1, 16 * (kch // 2))


def emit_l3_ffn(ctx, X, XLN, scr_w1t, scr_w2t):
    nc = ctx.nc
    NF = 32
    TBN = 256
    NTB = T // TBN
    with contextlib.ExitStack() as stack:
        w1b = ctx.sbuf([128, 2, NCH, 128], BF16, stack=stack, base="l3_w1b")
        w2b = ctx.sbuf([128, 2, 1024], BF16, stack=stack, base="l3_w2b")
        hbuf = ctx.sbuf([128, 2, TBN], BF16, stack=stack, base="l3_hb")
        b1c = ctx.sbuf([128, 32], F32, stack=stack, base="l3_b1c")
        b2c = ctx.sbuf([128, 8], F32, stack=stack, base="l3_b2c")
        ps_h = ctx.psum([128, 2, 512], F32, stack=stack, base="l3_psh")
        ps_y = ctx.psum([128, 4, 512], F32, stack=stack, base="l3_psy")
        sem_b = ctx.dsem(stack, "hw")
        wld = SlotSems(ctx, stack, 2, kind="sw")
        sem_hmm = ctx.sem(stack)
        sem_ymm = ctx.sem(stack)
        sem_ev = ctx.sem(stack)
        with nc.Block() as b:
            @b.sync
            def _(sy):
                sem_b.inc(sy.dma_start(out=b1c[:], in_=ctx.dram["l3_b1c"]))
                sem_b.inc(sy.dma_start(out=b2c[:], in_=ctx.dram["l3_b2c"]))
        sem_b.barrier(32)
        with nc.Block() as b:
            @b.vector
            def _(v):
                v.tensor_scalar_mul(out=b2c[:], in0=b2c[:], scalar1=0.5)
        w1src = scr_w1t.rearrange("(c p) f -> p c f", p=128)

        def ps_y_chunk(hc):
            return ps_y[:, hc // 2, (hc % 2) * TBN:(hc % 2 + 1) * TBN]
        for tb in range(NTB):
            t0 = tb * TBN
            with nc.Block() as b:
                @b.gpsimd
                def _(g):
                    for f in range(NF):
                        idx = tb * NF + f
                        if f >= 2:
                            g.wait_ge(sem_ymm, idx - 1)
                        wld.inc(g.dma_start(out=w1b[:, f % 2, :, :],
                                            in_=w1src[:, :, f * 128:(f + 1) * 128]), f % 2)
                        wld.inc(g.dma_start(out=w2b[:, f % 2, :],
                                            in_=scr_w2t[f * 128:(f + 1) * 128, :]), f % 2)
                @b.tensor
                def _(t):
                    for f in range(NF):
                        idx = tb * NF + f
                        wld.wait(t, f % 2, 32 * (idx // 2 + 1))
                        if f >= 2:
                            t.wait_ge(sem_ev, idx - 1)
                        for dc in range(NCH):
                            mm = t.matmul(ps_h[:, f % 2, 0:TBN], w1b[:, f % 2, dc, :],
                                          XLN[:, dc, t0:t0 + TBN],
                                          start=(dc == 0), stop=(dc == NCH - 1),
                                          skip_group_check=True)
                        mm.then_inc(sem_hmm, 1)
                        if f == 1:
                            for j in range(4):
                                t.matmul(ps_y[:, j, :], ctx.consts["ones_row_bf"][0:1, :],
                                         ctx.consts["zeros_row512"][0:1, :],
                                         start=True, stop=False, skip_group_check=True)
                        if f >= 1:
                            t.wait_ge(sem_ev, idx)
                            for hc in range(NCH):
                                mm = t.matmul(ps_y_chunk(hc),
                                              w2b[:, (f - 1) % 2, hc * 128:(hc + 1) * 128],
                                              hbuf[:, (f - 1) % 2, :],
                                              start=False, stop=False,
                                              skip_group_check=True)
                            mm.then_inc(sem_ymm, 1)
                    t.wait_ge(sem_ev, tb * NF + NF)
                    for hc in range(NCH):
                        mm = t.matmul(ps_y_chunk(hc),
                                      w2b[:, (NF - 1) % 2, hc * 128:(hc + 1) * 128],
                                      hbuf[:, (NF - 1) % 2, :],
                                      start=False, stop=True, skip_group_check=True)
                    mm.then_inc(sem_ymm, 1)
                @b.scalar
                def _(s):
                    for f in range(NF):
                        idx = tb * NF + f
                        s.wait_ge(sem_hmm, idx + 1)
                        if f >= 2:
                            s.wait_ge(sem_ymm, idx - 1)
                        s.activation(out=hbuf[:, f % 2, :], in_=ps_h[:, f % 2, 0:TBN],
                                     func=AF.Gelu, bias=b1c[:, f:f + 1]).then_inc(sem_ev, 1)
            with nc.Block() as b:
                @b.vector
                def _(v):
                    for hc in range(NCH):
                        v.scalar_tensor_tensor(out=X[:, hc, t0:t0 + TBN],
                                               in0=ps_y_chunk(hc), scalar=b2c[:, hc:hc + 1],
                                               in1=X[:, hc, t0:t0 + TBN],
                                               op0=ALU.add, op1=ALU.add)



def build_program(emit_body, arrays, n_cores=1):
    nc = bass.Bass("TRN2", target_bir_lowering=False, debug=False)
    with contextlib.ExitStack() as stack:
        ctx = Ctx(nc, stack)
        declare_inputs(ctx, arrays)
        xout = nc.dram_tensor("x_out", [128, NCH, T], F32, kind="ExternalOutput").ap()
        X = ctx.sbuf([128, NCH, T], F32, base="X")
        XLN = ctx.sbuf([128, NCH, T], BF16, base="XLN")
        sem_x = ctx.dsem(kind="hw")
        with nc.Block() as b:
            @b.sync
            def _(sy):
                sem_x.inc(sy.dma_start(out=X[:], in_=ctx.dram["x_fm"]))
        sem_x.barrier(16)
        ctx.register_fconsts([1e-5, 1e-6, 1e-12, 0.21, 0.8, 1e-8, 0.5, -0.5])
        load_consts(ctx, [k for k in base_consts() if k in arrays])
        emit_body(ctx, X, XLN)
        sem_o = ctx.dsem(kind="hw")
        with nc.Block() as b:
            @b.sync
            def _(sy):
                sem_o.inc(sy.dma_start(out=xout, in_=X[:]))
                sem_o.wait(sy, 16)
    return nc



def l1_ffn_host_arrays(weight, disha):
    P = np.zeros((D, 64), np.float32)
    for g in range(16):
        for r in range(64):
            P[g * 64 + r, r] = 1.0
    wT = np.ascontiguousarray(weight.T)
    return {
        "l1_wTr": np.ascontiguousarray(wT.reshape(NCH, 128, D).transpose(1, 0, 2)),
        "l1_dishab": np.ascontiguousarray(disha),
        "l1_poolr": np.ascontiguousarray(P.reshape(NCH, 128, 64).transpose(1, 0, 2)),
    }


L1_FFN_CONST_DTYPES = {"l1_wTr": BF16, "l1_dishab": BF16, "l1_poolr": BF16}


def emit_l1_ffn(ctx, X, XLN):
    nc = ctx.nc
    wT = ctx.consts["l1_wTr"]
    disha = ctx.consts["l1_dishab"]
    pool = ctx.consts["l1_poolr"]
    with contextlib.ExitStack() as stack:
        pooled_bf = ctx.sbuf([64, 512], BF16, stack=stack, base="l1_pool")
        ps_pool = ctx.psum([64, 512], F32, stack=stack, base="l1_psp")
        ps_y = ctx.psum([128, 4, 512], F32, stack=stack, base="l1_psy")
        for tb in range(4):
            t0 = tb * 512
            with nc.Block() as b:
                @b.tensor
                def _(t):
                    for dc in range(NCH):
                        t.matmul(ps_pool[:, :], pool[:, dc, :], XLN[:, dc, t0:t0 + 512],
                                 start=(dc == 0), stop=(dc == NCH - 1))
            with nc.Block() as b:
                @b.scalar
                def _(s):
                    s.activation(out=pooled_bf[:], in_=ps_pool[:], func=AF.Copy)
            for half in range(2):
                with nc.Block() as b:
                    @b.tensor
                    def _(t):
                        for j in range(4):
                            hc = half * 4 + j
                            for dc in range(NCH):
                                t.matmul(ps_y[:, j, :], wT[:, dc, hc * 128:(hc + 1) * 128],
                                         XLN[:, dc, t0:t0 + 512],
                                         start=(dc == 0), stop=False)
                            t.matmul(ps_y[:, j, :], disha[:, hc * 128:(hc + 1) * 128],
                                     pooled_bf[:, :], start=False, stop=True)
                with nc.Block() as b:
                    @b.vector
                    def _(v):
                        for j in range(4):
                            hc = half * 4 + j
                            v.scalar_tensor_tensor(out=X[:, hc, t0:t0 + 512],
                                                   in0=ps_y[:, j, :], scalar=0.5,
                                                   in1=X[:, hc, t0:t0 + 512],
                                                   op0=ALU.mult, op1=ALU.add)



def l0_ffn_host_arrays(Wr, br, W1, b1, W2, b2):
    ltm = np.zeros((128, 8, 8), np.float32)
    for e in range(8):
        ltm[:, e, :e] = 1.0
    return {
        "l0_wr": np.ascontiguousarray(Wr.reshape(NCH, 128, 8).transpose(1, 0, 2)),
        "l0_br": np.ascontiguousarray(br.reshape(1, 8)),
        "l0_b1c": np.ascontiguousarray(
            b1.reshape(8, 32, 128).transpose(2, 0, 1).reshape(128, 256)),
        "l0_b2r": np.ascontiguousarray(b2.reshape(1, 8 * 1024)),
        "ltmask": np.ascontiguousarray(ltm.reshape(128, 64)),
        "ident128": np.eye(128, dtype=np.float32),
        "l0_W1": W1, "l0_W2": W2,
    }


L0_FFN_CONST_DTYPES = {"l0_wr": BF16, "l0_br": BF16, "l0_b2r": BF16}
L0_FFN_CONSTS = ["l0_wr", "l0_br", "l0_b1c", "l0_b2r", "ltmask", "ident128"]



def emit_router_gate(ctx, XLN, gate_dram, wr_name, br_name, mode, scale, top_p=0.8):
    nc = ctx.nc
    wr = ctx.consts[wr_name]
    br = ctx.consts[br_name] if br_name else None
    ltm = ctx.consts["ltmask"]
    ident = ctx.consts["ident128"]
    with contextlib.ExitStack() as gstack:
        probs = ctx.sbuf([128, 16, 8], F32, stack=gstack, base="rg_probs")
        nmax = ctx.sbuf([128, 16], F32, stack=gstack, base="rg_nmax")
        sums = ctx.sbuf([128, 16], F32, stack=gstack, base="rg_sums")
        cmpA = ctx.sbuf([128, 16, 8], F32, stack=gstack, base="rg_cmpA")
        cmpB = ctx.sbuf([128, 16, 8], F32, stack=gstack, base="rg_cmpB")
        Scol = ctx.sbuf([128, 16, 8], F32, stack=gstack, base="rg_Scol")
        Cnt = ctx.sbuf([128, 16, 8], F32, stack=gstack, base="rg_Cnt")
        junk = ctx.sbuf([128, 16, 8], F32, stack=gstack, base="rg_junk")
        junk2 = ctx.sbuf([128, 16, 8], F32, stack=gstack, base="rg_junk2")
        gate_all = ctx.sbuf([8, T], F32, stack=gstack, base="rg_gall")
        ps_l = ctx.psum([128, 16, 8], F32, stack=gstack, base="rg_psl")
        ps_gt = ctx.psum([8, T], F32, stack=gstack, base="rg_psgt")
        es = ESync(ctx, gstack)
        with nc.Block() as b:
            @b.tensor
            def _(t):
                for tc in range(16):
                    for dc in range(NCH):
                        t.matmul(ps_l[:, tc, :], XLN[:, dc, tc * 128:(tc + 1) * 128],
                                 wr[:, dc, :], start=(dc == 0),
                                 stop=(dc == NCH - 1 and br is None))
                    if br is not None:
                        t.matmul(ps_l[:, tc, :], ctx.consts["ones_row_bf"][0:1, :],
                                 br[0:1, :], start=False, stop=True)
        with nc.Block() as b:
            @b.vector
            def _(v):
                for tc in range(16):
                    v.tensor_reduce(out=nmax[:, tc:tc + 1], in_=ps_l[:, tc, :],
                                    axis=AX.X, op=ALU.max, negate=True)
        with nc.Block() as b:
            @b.scalar
            def _(s):
                for tc in range(16):
                    s.activation(out=probs[:, tc, :], in_=ps_l[:, tc, :], func=AF.Exp,
                                 bias=nmax[:, tc:tc + 1], accum_out=sums[:, tc:tc + 1])
        with nc.Block() as b:
            @b.vector
            def _(v):
                v.reciprocal(out=sums[:], in_=sums[:])
        with nc.Block() as b:
            @b.vector
            def _(v):
                for tc in range(16):
                    v.tensor_scalar_mul(out=probs[:, tc, :], in0=probs[:, tc, :],
                                        scalar1=sums[:, tc:tc + 1])
        with nc.Block() as b:
            @b.vector
            def _(v):
                for e in range(8):
                    for tc in range(16):
                        v.tensor_scalar(out=cmpA[:, tc, :], in0=probs[:, tc, :],
                                        scalar1=probs[:, tc, e:e + 1], scalar2=None,
                                        op0=ALU.is_gt)
                    for tc in range(16):
                        i = v.tensor_scalar(out=cmpB[:, tc, :], in0=probs[:, tc, :],
                                            scalar1=probs[:, tc, e:e + 1], scalar2=None,
                                            op0=ALU.is_equal)
                    es.edge(v, i)
                    for tc in range(16):
                        i = v.tensor_mul(out=cmpB[:, tc, :], in0=cmpB[:, tc, :],
                                         in1=ltm[:, e * 8:(e + 1) * 8])
                    es.edge(v, i)
                    for tc in range(16):
                        i = v.tensor_add(out=cmpA[:, tc, :], in0=cmpA[:, tc, :],
                                         in1=cmpB[:, tc, :])
                    es.edge(v, i)
                    if mode == "topp":
                        for tc in range(16):
                            v.scalar_tensor_tensor(out=junk[:, tc, :], in0=cmpA[:, tc, :],
                                                   scalar=1.0, in1=probs[:, tc, :],
                                                   op0=ALU.mult, op1=ALU.mult,
                                                   accum_out=Scol[:, tc, e:e + 1])
                    for tc in range(16):
                        i = v.scalar_tensor_tensor(out=junk2[:, tc, :], in0=cmpA[:, tc, :],
                                                   scalar=1.0, in1=cmpA[:, tc, :],
                                                   op0=ALU.mult, op1=ALU.max,
                                                   accum_out=Cnt[:, tc, e:e + 1])
                    es.edge(v, i)
                if mode == "topp":
                    i = v.tensor_add(out=Scol[:], in0=Scol[:], in1=probs[:])
                    es.edge(v, i)
                    v.tensor_scalar(out=Scol[:], in0=Scol[:], scalar1=float(top_p),
                                    scalar2=None, op0=ALU.is_lt)
                    i = v.tensor_scalar(out=Cnt[:], in0=Cnt[:], scalar1=0.5,
                                        scalar2=None, op0=ALU.is_lt)
                    es.edge(v, i)
                    i = v.tensor_max(out=Scol[:], in0=Scol[:], in1=Cnt[:])
                    es.edge(v, i)
                    v.tensor_mul(out=Scol[:], in0=Scol[:], in1=probs[:])
                else:
                    i = v.tensor_scalar(out=Cnt[:], in0=Cnt[:], scalar1=1.5,
                                        scalar2=None, op0=ALU.is_lt)
                    es.edge(v, i)
                    i = v.tensor_mul(out=Scol[:], in0=Cnt[:], in1=probs[:])
                    es.edge(v, i)
                    for tc in range(16):
                        i = v.tensor_scalar(out=junk[:, tc, :], in0=Scol[:, tc, :],
                                            scalar1=1.0, scalar2=0.0, op0=ALU.mult,
                                            op1=ALU.add, accum_out=sums[:, tc:tc + 1])
                    es.edge(v, i)
                    i = v.reciprocal(out=sums[:], in_=sums[:])
                    es.edge(v, i)
                    for tc in range(16):
                        v.tensor_scalar_mul(out=Scol[:, tc, :], in0=Scol[:, tc, :],
                                            scalar1=sums[:, tc:tc + 1])
        if scale != 1.0:
            with nc.Block() as b:
                @b.vector
                def _(v):
                    v.tensor_scalar_mul(out=Scol[:], in0=Scol[:], scalar1=float(scale))
        with nc.Block() as b:
            @b.tensor
            def _(t):
                for tc in range(16):
                    t.transpose(ps_gt[0:8, tc * 128:(tc + 1) * 128],
                                Scol[:, tc, 0:8], ident[:, :])
        with nc.Block() as b:
            @b.vector
            def _(v):
                v.tensor_copy(out=gate_all[:], in_=ps_gt[0:8, :])
        sem_g = ctx.dsem(gstack, "sw")
        with nc.Block() as b:
            @b.gpsimd
            def _(g):
                sem_g.inc(g.dma_start(out=gate_dram, in_=gate_all[:]))
        sem_g.barrier(16)


def emit_l0_ffn(ctx, X, XLN, top_p=0.8):
    nc = ctx.nc
    W1 = ctx.dram["l0_W1"].rearrange("e (c p) f -> e p c f", p=128)
    W2 = ctx.dram["l0_W2"]
    wr = ctx.consts["l0_wr"]
    br = ctx.consts["l0_br"]
    b1c = ctx.consts["l0_b1c"]
    b2r = ctx.consts["l0_b2r"]
    ltm = ctx.consts["ltmask"]
    ident = ctx.consts["ident128"]
    gate_dram = nc.dram_tensor(ctx.name("l0_gate_scr"), [8, T], F32, kind="Internal").ap()
    emit_router_gate(ctx, XLN, gate_dram, "l0_wr", "l0_br", "topp", 0.5, top_p=top_p)
    with contextlib.ExitStack() as stack:
        grow = ctx.sbuf([1, T], F32, stack=stack, base="l0_gr")
        growb = ctx.sbuf([1, T], BF16, stack=stack, base="l0_grb")
        NF = 32
        hbuf = ctx.sbuf([128, NF, 1024], BF16, stack=stack, base="l0_hbuf")
        w1b = ctx.sbuf([128, 2, NCH, 128], BF16, stack=stack, base="l0_w1b")
        w2b = ctx.sbuf([128, 2, 512], BF16, stack=stack, base="l0_w2b")
        gate_sb = ctx.sbuf([128, 1024], F32, stack=stack, base="l0_gsb")
        tmps = ctx.sbuf([128, 2, 512], F32, stack=stack, base="l0_tmps")
        w1ld = SlotSems(ctx, stack, 2)
        w2ld = SlotSems(ctx, stack, 2)
        sem_hmm = ctx.sem(stack)
        sem_ev = ctx.sem(stack)
        sem_ymm = ctx.sem(stack)
        cnt = {"hmm": 0, "ev": 0, "ymm": 0, "w1": 0, "w2": 0}
        esv = ESync(ctx, stack)
        sem_gr = ctx.dsem(stack, "sw")
        gr_cnt = [0]
        for e in range(8):
            with nc.Block() as b:
                @b.gpsimd
                def _(g):
                    sem_gr.inc(g.dma_start(out=grow[:], in_=gate_dram[e:e + 1, :]))
                    sem_gr.inc(g.dma_start(out=growb[:], in_=gate_dram[e:e + 1, :]))
            gr_cnt[0] += 32
            sem_gr.barrier(gr_cnt[0])
            for ts in range(2):
                t0 = ts * 1024
                with contextlib.ExitStack() as ast:
                    ps_h = ctx.psum([128, 2, 1024], F32, stack=ast, base="l0_psh")
                    with nc.Block() as b:
                        @b.gpsimd
                        def _(g):
                            for f in range(NF):
                                if cnt["w1"] >= 2:
                                    g.wait_ge(sem_ev, cnt["w1"] - 1)
                                w1ld.inc(g.dma_start(out=w1b[:, f % 2, :, :],
                                                     in_=W1[e, :, :, f * 128:(f + 1) * 128]),
                                         f % 2)
                                cnt["w1"] += 1
                        @b.tensor
                        def _(t):
                            base_ev = cnt["ev"]
                            for f in range(NF):
                                w1ld.wait(t, f % 2, 16 * (cnt["hmm"] // 2 + 1))
                                if f >= 2:
                                    t.wait_ge(sem_ev, base_ev + f - 1)
                                for half in range(2):
                                    for dc in range(NCH):
                                        mm = t.matmul(ps_h[:, f % 2, half * 512:(half + 1) * 512],
                                                      w1b[:, f % 2, dc, :],
                                                      XLN[:, dc, t0 + half * 512:t0 + (half + 1) * 512],
                                                      start=(dc == 0), stop=(dc == NCH - 1),
                                                      skip_group_check=True)
                                mm.then_inc(sem_hmm, 1)
                                cnt["hmm"] += 1
                        @b.scalar
                        def _(s):
                            base = cnt["ev"]
                            for f in range(NF):
                                s.wait_ge(sem_hmm, base + f + 1)
                                s.activation(out=hbuf[:, f, :], in_=ps_h[:, f % 2, :],
                                             func=AF.Relu,
                                             bias=b1c[:, e * 32 + f:e * 32 + f + 1]).then_inc(sem_ev, 1)
                                cnt["ev"] += 1
                with contextlib.ExitStack() as gst:
                    ps_gb = ctx.psum([128, 1024], F32, stack=gst, base="l0_psgb")
                    with nc.Block() as b:
                        @b.tensor
                        def _(t):
                            for half in range(2):
                                t.matmul(ps_gb[:, half * 512:(half + 1) * 512],
                                         ctx.consts["ones_row_f32"][0:1, :],
                                         grow[0:1, t0 + half * 512:t0 + (half + 1) * 512],
                                         start=True, stop=True)
                    with nc.Block() as b:
                        @b.scalar
                        def _(s):
                            s.activation(out=gate_sb[:], in_=ps_gb[:], func=AF.Copy)
                for hg in range(2):
                    with contextlib.ExitStack() as bst:
                        ps_y = ctx.psum([128, 4, 1024], F32, stack=bst, base="l0_psy")
                        with nc.Block() as b:
                            @b.gpsimd
                            def _(g):
                                for f in range(NF):
                                    if cnt["w2"] >= 2:
                                        g.wait_ge(sem_ymm, cnt["w2"] - 1)
                                    w2ld.inc(g.dma_start(
                                        out=w2b[:, f % 2, :],
                                        in_=W2[e, f * 128:(f + 1) * 128,
                                               hg * 512:(hg + 1) * 512]), f % 2)
                                    cnt["w2"] += 1
                            @b.tensor
                            def _(t):
                                for f in range(NF):
                                    w2ld.wait(t, f % 2, 16 * (cnt["ymm"] // 2 + 1))
                                    for hc in range(4):
                                        for half in range(2):
                                            mm = t.matmul(
                                                ps_y[:, hc, half * 512:(half + 1) * 512],
                                                w2b[:, f % 2, hc * 128:(hc + 1) * 128],
                                                hbuf[:, f, half * 512:(half + 1) * 512],
                                                start=(f == 0), stop=False,
                                                skip_group_check=True)
                                    mm.then_inc(sem_ymm, 1)
                                    cnt["ymm"] += 1
                                for hc in range(4):
                                    gh = hg * 4 + hc
                                    for half in range(2):
                                        t.matmul(ps_y[:, hc, half * 512:(half + 1) * 512],
                                                 b2r[0:1, e * 1024 + gh * 128:e * 1024 + (gh + 1) * 128],
                                                 growb[0:1, t0 + half * 512:t0 + (half + 1) * 512],
                                                 start=False, stop=True,
                                                 skip_group_check=True)
                        with nc.Block() as b:
                            @b.vector
                            def _(v):
                                for half in range(2):
                                    h0 = half * 512
                                    for pair in range(2):
                                        for k in range(2):
                                            hc = pair * 2 + k
                                            i = v.tensor_mul(out=tmps[:, k, :],
                                                             in0=ps_y[:, hc, h0:h0 + 512],
                                                             in1=gate_sb[:, h0:h0 + 512])
                                        esv.edge(v, i)
                                        for k in range(2):
                                            hc = pair * 2 + k
                                            gh = hg * 4 + hc
                                            i = v.tensor_add(out=X[:, gh, t0 + h0:t0 + h0 + 512],
                                                             in0=X[:, gh, t0 + h0:t0 + h0 + 512],
                                                             in1=tmps[:, k, :])
                                        esv.edge(v, i)



def l2_ffn_host_arrays(gate_w, w1, b1, ln1_g, ln1_b, w2, b2, ln_g, ln_b):
    def colcat(a):
        return np.ascontiguousarray(a.reshape(8, 16, 128).transpose(2, 0, 1).reshape(128, 128))
    return {
        "l2_gw": np.ascontiguousarray(gate_w.reshape(NCH, 128, 8).transpose(1, 0, 2)),
        "l2_b1c": colcat(b1),
        "l2_ln1g": colcat(ln1_g),
        "l2_ln1b": colcat(ln1_b),
        "l2_b2r": np.ascontiguousarray(b2.reshape(1, 8 * 1024)),
        "l2_lng": mk_col8(ln_g),
        "l2_lnb": mk_col8(ln_b),
        "l2_w1": w1, "l2_w2": w2,
    }


def mk_col8(vec):
    return np.ascontiguousarray(vec.reshape(8, 128).T)


L2_FFN_CONST_DTYPES = {"l2_gw": BF16}
L2_FFN_CONSTS = ["l2_gw", "l2_b1c", "l2_ln1g", "l2_ln1b", "l2_lng", "l2_lnb"]


def emit_l2_ffn_prep(ctx, scr_w1, scr_w2, wbar_bf, bbar_bf):
    nc = ctx.nc
    W1 = ctx.dram["l2_w1"]
    W2 = ctx.dram["l2_w2"]
    with contextlib.ExitStack() as stack:
        wbar = ctx.sbuf([128, 8, 8], F32, stack=stack, base="l2p_wbar")
        bbar = ctx.sbuf([1, 8], F32, stack=stack, base="l2p_bbar")
        buf = ctx.sbuf([128, 2, 2048], BF16, stack=stack, base="l2p_buf")
        ld = SlotSems(ctx, stack, 2, kind="sw")
        st = SlotSems(ctx, stack, 2, kind="hw")
        es = ESync(ctx, stack)
        sem_vr = ctx.sem(stack)
        k = [0]
        with nc.Block() as b:
            @b.gpsimd
            def _(g):
                for e in range(8):
                    for dc in range(NCH):
                        s = k[0] % 2
                        if k[0] >= 2:
                            st.wait(g, s, 16 * ((k[0] - 2) // 2 + 1))
                            g.wait_ge(sem_vr, k[0] - 1)
                        ld.inc(g.dma_start(out=buf[:, s, :],
                                           in_=W1[e, dc * 128:(dc + 1) * 128, :]), s)
                        k[0] += 1
            @b.sync
            def _(sy):
                for e in range(8):
                    for dc in range(NCH):
                        kk = e * NCH + dc
                        s = kk % 2
                        ld.wait(sy, s, 16 * (kk // 2 + 1))
                        st.inc(sy.dma_start(out=scr_w1[e, dc * 128:(dc + 1) * 128, :],
                                            in_=buf[:, s, :]), s)
            @b.vector
            def _(v):
                for e in range(8):
                    for dc in range(NCH):
                        kk = e * NCH + dc
                        s = kk % 2
                        ld.wait(v, s, 16 * (kk // 2 + 1))
                        v.tensor_reduce(out=wbar[:, e, dc:dc + 1], in_=buf[:, s, :],
                                        axis=AX.X, op=ALU.add).then_inc(sem_vr, 1)
        st.barrier(0, 16 * 32)
        st.barrier(1, 16 * 32)
        with nc.Block() as b:
            @b.vector
            def _(v):
                v.tensor_scalar_mul(out=wbar[:, :, :], in0=wbar[:, :, :], scalar1=1.0 / 2048)
        k2 = [0]
        ld2 = SlotSems(ctx, stack, 2, kind="sw")
        st2 = SlotSems(ctx, stack, 2, kind="hw")
        with nc.Block() as b:
            @b.gpsimd
            def _(g):
                for e in range(8):
                    for fc in range(16):
                        s = k2[0] % 2
                        if k2[0] >= 2:
                            st2.wait(g, s, 16 * ((k2[0] - 2) // 2 + 1))
                        ld2.inc(g.dma_start(out=buf[:, s, 0:1024],
                                            in_=W2[e, fc * 128:(fc + 1) * 128, :]), s)
                        k2[0] += 1
            @b.sync
            def _(sy):
                for e in range(8):
                    for fc in range(16):
                        kk = e * 16 + fc
                        s = kk % 2
                        ld2.wait(sy, s, 16 * (kk // 2 + 1))
                        st2.inc(sy.dma_start(out=scr_w2[e, fc * 128:(fc + 1) * 128, :],
                                             in_=buf[:, s, 0:1024]), s)
        st2.barrier(0, 16 * 64)
        st2.barrier(1, 16 * 64)
        with contextlib.ExitStack() as ps_stack:
            ps = ctx.psum([1, 128], F32, stack=ps_stack, base="l2p_ps")
            with nc.Block() as b:
                @b.tensor
                def _(t):
                    t.matmul(ps[0:1, :], ctx.consts["ones_col_f32"][:, 0:1],
                             ctx.consts["l2_b1c"][:, :], start=True, stop=True)
            with nc.Block() as b:
                @b.vector
                def _(v):
                    i = v.tensor_reduce(out=bbar[0:1, 0:8],
                                        in_=ps.rearrange("a (e f) -> a e f", e=8)[0:1, :, :],
                                        axis=AX.X, op=ALU.add)
                    es.edge(v, i)
                    i = v.tensor_scalar_mul(out=bbar[:], in0=bbar[:], scalar1=1.0 / 2048)
                    es.edge(v, i)
                    i = v.tensor_copy(out=bbar_bf[:], in_=bbar[:])
                    v.tensor_copy(out=wbar_bf[:], in_=wbar[:])


def emit_l2_ffn(ctx, X, XLN, scr_w1, scr_w2, wbar_bf, bbar_bf):
    nc = ctx.nc
    gate_dram = nc.dram_tensor(ctx.name("l2_gate_scr"), [8, T], F32, kind="Internal").ap()
    emit_router_gate(ctx, XLN, gate_dram, "l2_gw", None, "top2", 1.0)
    NF = 16
    with contextlib.ExitStack() as stack:
        grow = ctx.sbuf([1, T], F32, stack=stack, base="l2_gr")
        growb = ctx.sbuf([1, T], BF16, stack=stack, base="l2_grb")
        Tb = ctx.sbuf([128, NCH, 1024], F32, stack=stack, base="l2_T")
        hm = ctx.sbuf([128, NF, 1024], BF16, stack=stack, base="l2_hm")
        x2db = ctx.sbuf([128, 2, 1024], BF16, stack=stack, base="l2_x2")
        t2db = ctx.sbuf([128, 2, 512], F32, stack=stack, base="l2_t2")
        w1b = ctx.sbuf([128, 2, NCH, 128], BF16, stack=stack, base="l2_w1b")
        w2b = ctx.sbuf([128, 2, 512], BF16, stack=stack, base="l2_w2b")
        murow = ctx.sbuf([1, 1024], F32, stack=stack, base="l2_mur")
        varrow = murow
        rb_sb = ctx.sbuf([128, 1024], F32, stack=stack, base="l2_rb")
        comb_sb = rb_sb
        tmps = ctx.sbuf([128, 2, 512], F32, stack=stack, base="l2_tmp")
        b1c = ctx.consts["l2_b1c"]
        g1c = ctx.consts["l2_ln1g"]
        bb1c = ctx.consts["l2_ln1b"]
        b2row = ctx.sbuf([1, 1024], BF16, stack=stack, base="l2_b2row")
        g05 = ctx.sbuf([128, 8], F32, stack=stack, base="l2_g05")
        b05 = ctx.sbuf([128, 8], F32, stack=stack, base="l2_b05")
        w1ld = SlotSems(ctx, stack, 2, kind="sw")
        w2ld = SlotSems(ctx, stack, 2, kind="sw")
        sem_gr = ctx.dsem(stack, "sw")
        sem_hmm = ctx.sem(stack)
        sem_hv = ctx.sem(stack)
        sem_sq = ctx.sem(stack)
        sem_smm = ctx.sem(stack)
        sem_bv = ctx.sem(stack)
        sem_bs = ctx.sem(stack)
        sem_ymm = ctx.sem(stack)
        cnt = {"w1": 0, "hmm": 0, "hv": 0, "sq": 0, "smm": 0, "bv": 0, "bs": 0,
               "w2": 0, "ymm": 0, "gr": 0}
        esv = ESync(ctx, stack)
        with nc.Block() as b:
            @b.vector
            def _(v):
                v.tensor_scalar_mul(out=g05[:], in0=ctx.consts["l2_lng"][:], scalar1=0.5)
                v.tensor_scalar_mul(out=b05[:], in0=ctx.consts["l2_lnb"][:], scalar1=0.5)
        W1s = scr_w1.rearrange("e (c p) f -> e p c f", p=128)
        for tb in range(2):
            t0 = tb * 1024
            with nc.Block() as b:
                @b.vector
                def _(v):
                    for c in range(NCH):
                        v.tensor_copy(out=Tb[:, c, :], in_=XLN[:, c, t0:t0 + 1024])
            for e in range(8):
                with nc.Block() as b:
                    @b.gpsimd
                    def _(g):
                        sem_gr.inc(g.dma_start(out=grow[:], in_=gate_dram[e:e + 1, :]))
                        sem_gr.inc(g.dma_start(out=growb[:], in_=gate_dram[e:e + 1, :]))
                        sem_gr.inc(g.dma_start(out=b2row[:], in_=ctx.dram["l2_b2r"][0:1, e * 1024:(e + 1) * 1024]))
                cnt["gr"] += 48
                sem_gr.barrier(cnt["gr"])
                with contextlib.ExitStack() as ms_:
                    ps_mu = ctx.psum([1, 1024], F32, stack=ms_, base="l2_psmu")
                    with nc.Block() as b:
                        @b.tensor
                        def _(t):
                            for half in range(2):
                                h0 = half * 512
                                for dc in range(NCH):
                                    t.matmul(ps_mu[0:1, h0:h0 + 512],
                                             wbar_bf[:, e, dc:dc + 1],
                                             XLN[:, dc, t0 + h0:t0 + h0 + 512],
                                             start=(dc == 0), stop=False,
                                             skip_group_check=True)
                                t.matmul(ps_mu[0:1, h0:h0 + 512],
                                         bbar_bf[0:1, e:e + 1],
                                         ctx.consts["ones_row512"][0:1, :],
                                         start=False, stop=True, skip_group_check=True)
                    with nc.Block() as b:
                        @b.vector
                        def _(v):
                            v.tensor_copy(out=murow[:], in_=ps_mu[0:1, :])
                with contextlib.ExitStack() as as_:
                    ps_h = ctx.psum([128, 2, 1024], F32, stack=as_, base="l2_psh")
                    ps_var = ctx.psum([1, 1024], F32, stack=as_, base="l2_psv")
                    base_hmm = cnt["hmm"]; base_hv = cnt["hv"]
                    base_sq = cnt["sq"]; base_smm = cnt["smm"]
                    with nc.Block() as b:
                        @b.gpsimd
                        def _(g):
                            for f in range(NF):
                                if cnt["w1"] >= 2:
                                    g.wait_ge(sem_hmm, cnt["w1"] - 1)
                                w1ld.inc(g.dma_start(out=w1b[:, f % 2, :, :],
                                                     in_=W1s[e, :, :, f * 128:(f + 1) * 128]),
                                         f % 2)
                                cnt["w1"] += 1
                        @b.tensor
                        def _(t):
                            for f in range(NF):
                                w1ld.wait(t, f % 2, 16 * (cnt["hmm"] // 2 + 1))
                                if f >= 2:
                                    t.wait_ge(sem_hv, base_hv + f - 1)
                                for half in range(2):
                                    h0 = half * 512
                                    for dc in range(NCH):
                                        t.matmul(ps_h[:, f % 2, h0:h0 + 512],
                                                 w1b[:, f % 2, dc, :],
                                                 XLN[:, dc, t0 + h0:t0 + h0 + 512],
                                                 start=(dc == 0), stop=False,
                                                 skip_group_check=True)
                                    mm = t.matmul(ps_h[:, f % 2, h0:h0 + 512],
                                                  ctx.consts["negones_row_f32"][0:1, :],
                                                  murow[0:1, h0:h0 + 512],
                                                  start=False, stop=True,
                                                  skip_group_check=True)
                                mm.then_inc(sem_hmm, 1)
                                cnt["hmm"] += 1
                                if f >= 1:
                                    t.wait_ge(sem_sq, base_sq + f)
                                    for half in range(2):
                                        mm = t.matmul(ps_var[0:1, half * 512:(half + 1) * 512],
                                                      ctx.consts["ones_col_bf"][:, 0:1],
                                                      x2db[:, (f - 1) % 2, half * 512:(half + 1) * 512],
                                                      start=(f == 1), stop=False,
                                                      skip_group_check=True)
                                    mm.then_inc(sem_smm, 1)
                                    cnt["smm"] += 1
                            t.wait_ge(sem_sq, base_sq + NF)
                            for half in range(2):
                                mm = t.matmul(ps_var[0:1, half * 512:(half + 1) * 512],
                                              ctx.consts["ones_col_bf"][:, 0:1],
                                              x2db[:, (NF - 1) % 2, half * 512:(half + 1) * 512],
                                              start=False, stop=True, skip_group_check=True)
                            mm.then_inc(sem_smm, 1)
                            cnt["smm"] += 1
                        @b.vector
                        def _(v):
                            for f in range(NF):
                                v.wait_ge(sem_hmm, base_hmm + f + 1)
                                v.tensor_copy(out=hm[:, f, :],
                                              in_=ps_h[:, f % 2, :]).then_inc(sem_hv, 1)
                                cnt["hv"] += 1
                        @b.scalar
                        def _(s):
                            for f in range(NF):
                                s.wait_ge(sem_hv, base_hv + f + 1)
                                if f >= 2:
                                    s.wait_ge(sem_smm, base_smm + f - 1)
                                s.activation(out=x2db[:, f % 2, :], in_=hm[:, f, :],
                                             func=AF.Square).then_inc(sem_sq, 1)
                                cnt["sq"] += 1
                    es2 = ESync(ctx, as_)
                    with nc.Block() as b:
                        @b.vector
                        def _(v):
                            v.tensor_scalar_mul(out=varrow[:], in0=ps_var[0:1, :],
                                                scalar1=1.0 / 2048)
                    with nc.Block() as b:
                        @b.scalar
                        def _(s):
                            i = s.activation(out=varrow[:], in_=varrow[:], func=AF.Ln,
                                             bias=1e-5)
                            es2.edge(s, i)
                            s.activation(out=varrow[:], in_=varrow[:], func=AF.Exp,
                                         scale=-0.5)
                    with nc.Block() as b:
                        @b.tensor
                        def _(t):
                            for half in range(2):
                                t.matmul(ps_h[:, 0, half * 512:(half + 1) * 512],
                                         ctx.consts["ones_row_f32"][0:1, :],
                                         varrow[0:1, half * 512:(half + 1) * 512],
                                         start=True, stop=True, skip_group_check=True)
                    with nc.Block() as b:
                        @b.scalar
                        def _(s):
                            s.activation(out=rb_sb[:], in_=ps_h[:, 0, :], func=AF.Copy)
                base_bv = cnt["bv"]; base_bs = cnt["bs"]
                with nc.Block() as b:
                    @b.vector
                    def _(v):
                        for fh in range(2 * NF):
                            f, half = fh // 2, fh % 2
                            h0 = half * 512
                            if fh >= 2:
                                v.wait_ge(sem_bs, base_bs + fh - 1)
                            v.tensor_mul(out=t2db[:, fh % 2, :], in0=hm[:, f, h0:h0 + 512],
                                         in1=rb_sb[:, h0:h0 + 512]).then_inc(sem_bv, 1)
                            cnt["bv"] += 1
                    @b.scalar
                    def _(s):
                        for fh in range(2 * NF):
                            f, half = fh // 2, fh % 2
                            h0 = half * 512
                            s.wait_ge(sem_bv, base_bv + fh + 1)
                            s.activation(out=hm[:, f, h0:h0 + 512], in_=t2db[:, fh % 2, :],
                                         func=AF.Gelu, bias=bb1c[:, e * 16 + f:e * 16 + f + 1],
                                         scale=g1c[:, e * 16 + f:e * 16 + f + 1]).then_inc(sem_bs, 1)
                            cnt["bs"] += 1
                with contextlib.ExitStack() as cs_:
                    ps_gb = ctx.psum([128, 1024], F32, stack=cs_, base="l2_psgb")
                    with nc.Block() as b:
                        @b.tensor
                        def _(t):
                            for half in range(2):
                                t.matmul(ps_gb[:, half * 512:(half + 1) * 512],
                                         ctx.consts["ones_row_f32"][0:1, :],
                                         grow[0:1, t0 + half * 512:t0 + half * 512 + 512],
                                         start=True, stop=True)
                    with nc.Block() as b:
                        @b.scalar
                        def _(s):
                            s.activation(out=comb_sb[:], in_=ps_gb[:], func=AF.Copy)
                for hg in range(2):
                    with contextlib.ExitStack() as bs_:
                        ps_y = ctx.psum([128, 4, 1024], F32, stack=bs_, base="l2_psy")
                        base_ymm = cnt["ymm"]
                        with nc.Block() as b:
                            @b.gpsimd
                            def _(g):
                                for f in range(NF):
                                    if cnt["w2"] >= 2:
                                        g.wait_ge(sem_ymm, cnt["w2"] - 1)
                                    w2ld.inc(g.dma_start(
                                        out=w2b[:, f % 2, :],
                                        in_=scr_w2[e, f * 128:(f + 1) * 128,
                                                   hg * 512:(hg + 1) * 512]), f % 2)
                                    cnt["w2"] += 1
                            @b.tensor
                            def _(t):
                                for f in range(NF):
                                    w2ld.wait(t, f % 2, 16 * (cnt["ymm"] // 2 + 1))
                                    for hc in range(4):
                                        for half in range(2):
                                            mm = t.matmul(
                                                ps_y[:, hc, half * 512:(half + 1) * 512],
                                                w2b[:, f % 2, hc * 128:(hc + 1) * 128],
                                                hm[:, f, half * 512:(half + 1) * 512],
                                                start=(f == 0), stop=False,
                                                skip_group_check=True)
                                    mm.then_inc(sem_ymm, 1)
                                    cnt["ymm"] += 1
                                for hc in range(4):
                                    gh = hg * 4 + hc
                                    for half in range(2):
                                        t.matmul(ps_y[:, hc, half * 512:(half + 1) * 512],
                                                 b2row[0:1, gh * 128:(gh + 1) * 128],
                                                 growb[0:1, t0 + half * 512:t0 + half * 512 + 512],
                                                 start=False, stop=True,
                                                 skip_group_check=True)
                        with nc.Block() as b:
                            @b.vector
                            def _(v):
                                for half in range(2):
                                    h0 = half * 512
                                    for pair in range(2):
                                        for k in range(2):
                                            hc = pair * 2 + k
                                            i = v.tensor_mul(out=tmps[:, k, :],
                                                             in0=ps_y[:, hc, h0:h0 + 512],
                                                             in1=comb_sb[:, h0:h0 + 512])
                                        esv.edge(v, i)
                                        for k in range(2):
                                            hc = pair * 2 + k
                                            gh = hg * 4 + hc
                                            i = v.tensor_add(out=Tb[:, gh, h0:h0 + 512],
                                                             in0=Tb[:, gh, h0:h0 + 512],
                                                             in1=tmps[:, k, :])
                                        esv.edge(v, i)
            _emit_l2_final_ln(ctx, X, Tb, t0, x2db, g05, b05, hm, murow, grow)


def _emit_l2_final_ln(ctx, X, Tb, t0, x2db, g05, b05, hm, murow, grow, eps=1e-5):
    nc = ctx.nc
    with contextlib.ExitStack() as stack:
        tbf = hm[:, 0:NCH, :]
        mu2 = murow[0:1, :]
        ms2 = grow[0:1, 0:1024]
        r2 = grow[0:1, 1024:2048]
        es = ESync(ctx, stack)
        sem_c = ctx.sem(stack)
        with nc.Block() as b:
            @b.vector
            def _(v):
                for c in range(NCH):
                    v.tensor_copy(out=tbf[:, c, :], in_=Tb[:, c, :])
        with contextlib.ExitStack() as pstack:
            ps_sx = ctx.psum([1, 1024], F32, stack=pstack, base="l2f_psx")
            ps_sx2 = ctx.psum([1, 1024], F32, stack=pstack, base="l2f_psx2")
            sem_sq = ctx.sem(pstack)
            sem_mm = ctx.sem(pstack)
            c_sq = [0]
            c_mm = [0]
            with nc.Block() as b:
                @b.scalar
                def _(s):
                    for c in range(NCH):
                        if c >= 2:
                            s.wait_ge(sem_mm, c - 1)
                        s.activation(out=x2db[:, c % 2, :], in_=Tb[:, c, :],
                                     func=AF.Square).then_inc(sem_sq, 1)
                @b.tensor
                def _(t):
                    oc = ctx.consts["ones_col_bf"]
                    for c in range(NCH):
                        t.wait_ge(sem_sq, c + 1)
                        for half in range(2):
                            h0 = half * 512
                            t.matmul(ps_sx[0:1, h0:h0 + 512], oc[:, 0:1],
                                     tbf[:, c, h0:h0 + 512],
                                     start=(c == 0), stop=(c == NCH - 1),
                                     skip_group_check=True)
                            mm = t.matmul(ps_sx2[0:1, h0:h0 + 512], oc[:, 0:1],
                                          x2db[:, c % 2, h0:h0 + 512],
                                          start=(c == 0), stop=(c == NCH - 1),
                                          skip_group_check=True)
                        mm.then_inc(sem_mm, 1)
            with nc.Block() as b:
                @b.vector
                def _(v):
                    v.tensor_scalar_mul(out=mu2[:], in0=ps_sx[0:1, :], scalar1=1.0 / D)
                    v.tensor_scalar_mul(out=ms2[:], in0=ps_sx2[0:1, :], scalar1=1.0 / D)
        with nc.Block() as b:
            @b.vector
            def _(v):
                i = v.tensor_mul(out=r2[:], in0=mu2[:], in1=mu2[:])
                es.edge(v, i)
                v.tensor_sub(out=ms2[:], in0=ms2[:], in1=r2[:])
        with nc.Block() as b:
            @b.scalar
            def _(s):
                i = s.activation(out=ms2[:], in_=ms2[:], func=AF.Ln, bias=float(eps))
                es.edge(s, i)
                s.activation(out=r2[:], in_=ms2[:], func=AF.Exp, scale=-0.5)
        with contextlib.ExitStack() as pstack:
            ps_mb = ctx.psum([128, 1024], F32, stack=pstack, base="l2f_mb")
            ps_rb = ctx.psum([128, 1024], F32, stack=pstack, base="l2f_rb")
            with nc.Block() as b:
                @b.tensor
                def _(t):
                    orow = ctx.consts["ones_row_f32"]
                    for half in range(2):
                        h0 = half * 512
                        t.matmul(ps_mb[:, h0:h0 + 512], orow[0:1, :], mu2[0:1, h0:h0 + 512],
                                 start=True, stop=True)
                        t.matmul(ps_rb[:, h0:h0 + 512], orow[0:1, :], r2[0:1, h0:h0 + 512],
                                 start=True, stop=True)
            with nc.Block() as b:
                @b.vector
                def _(v):
                    for c in range(NCH):
                        i = v.tensor_sub(out=Tb[:, c, :], in0=Tb[:, c, :], in1=ps_mb[:, :])
                    es.edge(v, i)
                    for c in range(NCH):
                        i = v.tensor_mul(out=Tb[:, c, :], in0=Tb[:, c, :], in1=ps_rb[:, :])
                    es.edge(v, i)
                    for c in range(NCH):
                        i = v.scalar_tensor_tensor(out=X[:, c, t0:t0 + 1024],
                                                   in0=Tb[:, c, :], scalar=g05[:, c:c + 1],
                                                   in1=X[:, c, t0:t0 + 1024],
                                                   op0=ALU.mult, op1=ALU.add)
                    es.edge(v, i)
                    for c in range(NCH):
                        v.tensor_scalar_add(out=X[:, c, t0:t0 + 1024],
                                            in0=X[:, c, t0:t0 + 1024],
                                            scalar1=b05[:, c:c + 1])



def l3_attn_host_arrays(wdq, wdkv, wuq, wuk, wuv, wout, ln_s, ln_b):
    DR = 64
    base = 10000.0
    inv = 1.0 / base ** (np.arange(0, DR, 2, dtype=np.float32) / DR)
    tt = np.arange(T, dtype=np.float32)
    fr = np.outer(tt, inv)
    cos = np.concatenate([np.cos(fr), np.cos(fr)], -1)[:, ::2]
    sin = np.concatenate([np.sin(fr), np.sin(fr)], -1)[:, ::2]
    C1 = np.zeros((128, T), np.float32)
    C2 = np.zeros((128, T), np.float32)
    for r in range(128):
        i = (r % 64) // 2
        C1[r] = cos[:, i]
        C2[r] = (-sin[:, i]) if (r % 2 == 0) else sin[:, i]
    pswap = np.zeros((128, 128), np.float32)
    for k in range(128):
        pswap[k, k ^ 1] = 1.0
    def rearr(wT):
        K, M = wT.shape
        return np.ascontiguousarray(wT.reshape(K // 128, 128, M).transpose(1, 0, 2))
    return {
        "l3a_wdqT": rearr(np.ascontiguousarray(wdq.T)),
        "l3a_wdkvT": rearr(np.ascontiguousarray(wdkv.T)),
        "l3a_wuqT": rearr(np.ascontiguousarray(wuq.T)),
        "l3a_wukT": rearr(np.ascontiguousarray(wuk.T)),
        "l3a_wuvT": rearr(np.ascontiguousarray(wuv.T)),
        "l3a_woutT": rearr(np.ascontiguousarray(wout.T)),
        "l3a_c1": np.ascontiguousarray(C1), "l3a_c2": np.ascontiguousarray(C2),
        "l3a_pswap": pswap,
        "l3a_lnsc": mk_col8(ln_s), "l3a_lnbc": mk_col8(ln_b),
    }


L3_ATTN_CONST_DTYPES = {"l3a_wdqT": BF16, "l3a_wdkvT": BF16, "l3a_wuqT": BF16,
                        "l3a_wukT": BF16, "l3a_wuvT": BF16,
                        "l3a_c1": BF16, "l3a_c2": BF16, "l3a_pswap": BF16}
L3_ATTN_CONSTS = ["l3a_lnsc", "l3a_lnbc"]


def emit_l3_attn(ctx, X, XLN):
    nc = ctx.nc
    u_dram = nc.dram_tensor(ctx.name("l3a_uscr"), [128, NCH, T], BF16, kind="Internal").ap()
    with contextlib.ExitStack() as stack:
        q_r = XLN
        k_r = ctx.sbuf([128, NCH, T], BF16, stack=stack, base="l3a_kr")
        cq = ctx.sbuf([128, 4, T], BF16, stack=stack, base="l3a_cq")
        ckv = ctx.sbuf([128, 4, T], BF16, stack=stack, base="l3a_ckv")
        att = q_r
        esv = ESync(ctx, stack)
        sem_sp = ctx.dsem(stack, "hw")
        with nc.Block() as b:
            @b.sync
            def _(sy):
                sem_sp.inc(sy.dma_start(out=u_dram, in_=XLN[:]))
        sem_sp.barrier(16)
        with contextlib.ExitStack() as p1:
            mu3 = ctx.sbuf([1, T], BF16, stack=p1, base="l3a_mu")
            r3 = ctx.sbuf([1, T], BF16, stack=p1, base="l3a_r")
            _ln_stats_rows(ctx, XLN, mu3, r3, eps=1e-5)
            wdq = _load_const_scoped(ctx, p1, "l3a_wdqT", BF16)
            wdkv = _load_const_scoped(ctx, p1, "l3a_wdkvT", BF16)
            hblk = ctx.sbuf([128, NCH, 512], BF16, stack=p1, base="l3b_h")
            tmp = ctx.sbuf([128, 2, 512], F32, stack=p1, base="l3b_tmp")
            lnsc = ctx.consts["l3a_lnsc"]
            lnbc = ctx.consts["l3a_lnbc"]
            es = ESync(ctx, p1)
            for tb in range(4):
                t0 = tb * 512
                with contextlib.ExitStack() as pst:
                    ps_b = ctx.psum([128, 2, 512], F32, stack=pst, base="l3b_psb")
                    with nc.Block() as b:
                        @b.tensor
                        def _(t):
                            orow = ctx.consts["ones_row_bf"]
                            t.matmul(ps_b[:, 0, :], orow[0:1, :], mu3[0:1, t0:t0 + 512],
                                     start=True, stop=True)
                            t.matmul(ps_b[:, 1, :], orow[0:1, :], r3[0:1, t0:t0 + 512],
                                     start=True, stop=True)
                    with nc.Block() as b:
                        @b.vector
                        def _(v):
                            for cp in range(4):
                                for k in range(2):
                                    c = cp * 2 + k
                                    i = v.tensor_sub(out=tmp[:, k, :],
                                                     in0=XLN[:, c, t0:t0 + 512],
                                                     in1=ps_b[:, 0, :])
                                es.edge(v, i)
                                for k in range(2):
                                    i = v.tensor_mul(out=tmp[:, k, :], in0=tmp[:, k, :],
                                                     in1=ps_b[:, 1, :])
                                es.edge(v, i)
                                for k in range(2):
                                    c = cp * 2 + k
                                    i = v.tensor_scalar(out=hblk[:, c, :], in0=tmp[:, k, :],
                                                        scalar1=lnsc[:, c:c + 1],
                                                        scalar2=lnbc[:, c:c + 1],
                                                        op0=ALU.mult, op1=ALU.add)
                                es.edge(v, i)
                with contextlib.ExitStack() as pst:
                    ps_c = ctx.psum([128, 2, 512], F32, stack=pst, base="l3b_psc")
                    for lc in range(4):
                        with nc.Block() as b:
                            @b.tensor
                            def _(t):
                                for dc in range(NCH):
                                    t.matmul(ps_c[:, 0, :], wdq[:, dc, lc * 128:(lc + 1) * 128],
                                             hblk[:, dc, :], start=(dc == 0), stop=(dc == NCH - 1))
                                for dc in range(NCH):
                                    t.matmul(ps_c[:, 1, :], wdkv[:, dc, lc * 128:(lc + 1) * 128],
                                             hblk[:, dc, :], start=(dc == 0), stop=(dc == NCH - 1))
                        with nc.Block() as b:
                            @b.scalar
                            def _(s):
                                s.activation(out=cq[:, lc, t0:t0 + 512], in_=ps_c[:, 0, :],
                                             func=AF.Copy)
                            @b.vector
                            def _(v):
                                v.tensor_copy(out=ckv[:, lc, t0:t0 + 512], in_=ps_c[:, 1, :])
        with contextlib.ExitStack() as p2:
            wuq = _load_const_scoped(ctx, p2, "l3a_wuqT", BF16)
            wuk = _load_const_scoped(ctx, p2, "l3a_wukT", BF16)
            c1c = _load_const_scoped(ctx, p2, "l3a_c1", BF16)
            c2c = _load_const_scoped(ctx, p2, "l3a_c2", BF16)
            psw = _load_const_scoped(ctx, p2, "l3a_pswap", BF16)
            qraw = ctx.sbuf([128, 2, 512], BF16, stack=p2, base="l3b_qraw")
            mm12 = ctx.sbuf([128, 2, 512], F32, stack=p2, base="l3b_m12")
            sem_qm = ctx.sem(p2)
            sem_qe = ctx.sem(p2)
            sem_sw = ctx.sem(p2)
            sem_ro = ctx.sem(p2)
            c2_ = {"qm": 0, "qe": 0, "sw": 0, "ro": 0}
            es = ESync(ctx, p2)
            with contextlib.ExitStack() as pst:
                ps_q = ctx.psum([128, 2, 2, 512], F32, stack=pst, base="l3b_psq")
                with nc.Block() as b:
                    @b.tensor
                    def _(t):
                        for wi, (wc, csrc) in enumerate(((wuq, cq), (wuk, ckv))):
                            for tb in range(4):
                                t0 = tb * 512
                                for oc in range(NCH):
                                    n = c2_["qm"]
                                    if n >= 2:
                                        t.wait_ge(sem_qe, n - 1)
                                    for lc in range(4):
                                        mm = t.matmul(ps_q[:, n % 2, 0, :],
                                                      wc[:, lc, oc * 128:(oc + 1) * 128],
                                                      csrc[:, lc, t0:t0 + 512],
                                                      start=(lc == 0), stop=(lc == 3),
                                                      skip_group_check=True)
                                    mm.then_inc(sem_qm, 1)
                                    t.wait_ge(sem_qe, n + 1)
                                    if n >= 2:
                                        t.wait_ge(sem_ro, n - 1)
                                    t.matmul(ps_q[:, n % 2, 1, :], psw[:, :],
                                             qraw[:, n % 2, :], start=True, stop=True,
                                             skip_group_check=True).then_inc(sem_sw, 1)
                                    c2_["qm"] += 1
                    @b.scalar
                    def _(s):
                        for wi in range(2):
                            for tb in range(4):
                                for oc in range(NCH):
                                    n = c2_["qe"]
                                    s.wait_ge(sem_qm, n + 1)
                                    if n >= 2:
                                        s.wait_ge(sem_sw, n - 1)
                                        s.wait_ge(sem_ro, n - 1)
                                    s.activation(out=qraw[:, n % 2, :], in_=ps_q[:, n % 2, 0, :],
                                                 func=AF.Copy).then_inc(sem_qe, 1)
                                    c2_["qe"] += 1
                    @b.vector
                    def _(v):
                        for wi, dst in enumerate((q_r, k_r)):
                            for tb in range(4):
                                t0 = tb * 512
                                for oc in range(NCH):
                                    n = c2_["ro"]
                                    v.wait_ge(sem_sw, n + 1)
                                    i = v.tensor_mul(out=mm12[:, 0, :], in0=qraw[:, n % 2, :],
                                                     in1=c1c[:, t0:t0 + 512])
                                    i = v.tensor_mul(out=mm12[:, 1, :], in0=ps_q[:, n % 2, 1, :],
                                                     in1=c2c[:, t0:t0 + 512])
                                    es.edge(v, i)
                                    i = v.tensor_add(out=dst[:, oc, t0:t0 + 512],
                                                     in0=mm12[:, 0, :], in1=mm12[:, 1, :])
                                    es.edge(v, i)
                                    v.sem_inc(sem_ro, 1)
                                    c2_["ro"] += 1
        with contextlib.ExitStack() as p3:
            wuv = _load_const_scoped(ctx, p3, "l3a_wuvT", BF16)
            vt_h = ctx.sbuf([128, 16, 65], BF16, stack=p3, base="l3a_vth")
            q2s = ctx.sbuf([128, T], BF16, stack=p3, base="l3a_q2s")
            a_db = ctx.sbuf([128, 2, 512], BF16, stack=p3, base="l3a_adb")
            crow = ctx.sbuf([1, T], F32, stack=p3, base="l3a_crow")
            rrow = ctx.sbuf([128, 512], F32, stack=p3, base="l3a_rrow")
            rb_s = ctx.sbuf([64, 512], F32, stack=p3, base="l3a_rbs")
            sk1 = ctx.sbuf([1, 1], F32, stack=p3, base="l3a_sk")
            cnts = {"sp": 0, "ex": 0, "av": 0, "nr": 0, "se": 0, "vv": 0, "mv": 0}
            sems = {k: ctx.sem(p3) for k in ("sp", "ex", "av", "nr", "attw", "mv")}
            with nc.Block() as b:
                @b.vector
                def _(v):
                    v.memset(vt_h[:, :, 64:65], 1.0)
            for h in range(16):
                _l3_head(ctx, q_r, k_r, cq, ckv, wuv, vt_h, att, a_db, crow, rrow,
                         rb_s, sk1, q2s, h, sems, cnts, esv)
        with contextlib.ExitStack() as p4:
            woutT = _load_const_scoped(ctx, p4, "l3a_woutT", BF16)
            tmps = ctx.sbuf([128, 4, 512], F32, stack=p4, base="l3a_tm4")
            ublk = ctx.sbuf([128, 2, NCH, 512], BF16, stack=p4, base="l3a_ub")
            uld = SlotSems(ctx, p4, 2, kind="hw")
            es = ESync(ctx, p4)
            for tb in range(4):
                t0 = tb * 512
                with contextlib.ExitStack() as pst:
                    ps_o = ctx.psum([128, 4, 512], F32, stack=pst, base="l3a_pso")
                    with nc.Block() as b:
                        @b.sync
                        def _(sy):
                            uld.inc(sy.dma_start(out=ublk[:, tb % 2, :, :],
                                                 in_=u_dram[:, :, t0:t0 + 512]), tb % 2)
                    for og in range(2):
                        with nc.Block() as b:
                            @b.tensor
                            def _(t):
                                for j in range(4):
                                    oc = og * 4 + j
                                    for ec in range(NCH):
                                        t.matmul(ps_o[:, j, :],
                                                 woutT[:, ec, oc * 128:(oc + 1) * 128],
                                                 att[:, ec, t0:t0 + 512],
                                                 start=(ec == 0), stop=(ec == NCH - 1))
                        with nc.Block() as b:
                            @b.vector
                            def _(v):
                                uld.wait(v, tb % 2, 16 * (tb // 2 + 1))
                                for j in range(4):
                                    oc = og * 4 + j
                                    i = v.tensor_add(out=tmps[:, j, :],
                                                     in0=ublk[:, tb % 2, oc, :],
                                                     in1=ps_o[:, j, :])
                                es.edge(v, i)
                                for j in range(4):
                                    oc = og * 4 + j
                                    i = v.tensor_add(out=X[:, oc, t0:t0 + 512],
                                                     in0=X[:, oc, t0:t0 + 512],
                                                     in1=tmps[:, j, :])
                                es.edge(v, i)


def _load_const_scoped(ctx, stack, name, dtype):
    nc = ctx.nc
    dr = ctx.dram[name]
    sb = ctx.sbuf(list(dr.shape), dtype, stack=stack, base="sc_" + name)
    with contextlib.ExitStack() as ss:
        sem = ctx.dsem(ss, "sw")
        with nc.Block() as b:
            @b.gpsimd
            def _(g):
                sem.inc(g.dma_start(out=sb[:], in_=dr))
        sem.barrier(16)
    return sb


def _l3_head(ctx, q_r, k_r, cq, ckv, wuv, vt_h, att, a_db, crow, rrow,
             rb_s, sk1, q2s, h, sems, cnts, esv):
    nc = ctx.nc
    g, rb = h // 2, (h % 2) * 64
    odd = (h % 2 == 1)
    with contextlib.ExitStack() as st:
        ps_v = ctx.psum([128, 2, 512], F32, stack=st, base="l3h_psv")
        for tc2 in range(8):
            with nc.Block() as b:
                @b.tensor
                def _(t):
                    for k in range(2):
                        tc = tc2 * 2 + k
                        for lc in range(4):
                            t.matmul(ps_v[:, k, 0:64], ckv[:, lc, tc * 128:(tc + 1) * 128],
                                     wuv[:, lc, h * 64:(h + 1) * 64],
                                     start=(lc == 0), stop=(lc == 3),
                                     skip_group_check=True)
            with nc.Block() as b:
                @b.vector
                def _(v):
                    for k in range(2):
                        tc = tc2 * 2 + k
                        v.tensor_copy(out=vt_h[:, tc, 0:64], in_=ps_v[:, k, 0:64])
    with contextlib.ExitStack() as st:
        ps_n = ctx.psum([1, T], F32, stack=st, base="l3h_psn")
        for (src_t, is_k) in ((k_r, True), (q_r, False)):
            with nc.Block() as b:
                @b.scalar
                def _(s):
                    s.activation(out=q2s[rb:rb + 64, :], in_=src_t[rb:rb + 64, g, :],
                                 func=AF.Square)
            with nc.Block() as b:
                @b.tensor
                def _(t):
                    oc = ctx.consts["ones_col_bf"]
                    for nb in range(4):
                        n0 = nb * 512
                        t.matmul(ps_n[0:1, n0:n0 + 512], oc[rb:rb + 64, 0:1],
                                 q2s[rb:rb + 64, n0:n0 + 512],
                                 start=True, stop=True, skip_group_check=True)
            with nc.Block() as b:
                @b.vector
                def _(v):
                    if is_k:
                        v.reduce_max(out=sk1[0:1, 0:1], in_=ps_n[0:1, :], axis=AX.X)
                    else:
                        v.tensor_scalar_mul(out=crow[:], in0=ps_n[0:1, :],
                                            scalar1=sk1[0:1, 0:1])
        with nc.Block() as b:
            @b.scalar
            def _(s):
                s.activation(out=crow[:], in_=crow[:], func=AF.Sqrt)
    with contextlib.ExitStack() as st:
        ps_s = ctx.psum([128, 2, 512], F32, stack=st, base="l3h_pss")
        ps_av = ctx.psum([128, 2, 512], F32, stack=st, base="l3h_psav")
        ps_rb = ctx.psum([128, 512], F32, stack=st, base="l3h_psrb")
        tmp0f = ctx.sbuf([128, 512], F32, stack=st, base="l3h_tmp0")
        with nc.Block() as b:
            @b.tensor
            def _(t):
                for t1b in range(4):
                    t10 = t1b * 512
                    idx = cnts["nr"]
                    for t2c in range(16):
                        sp = cnts["sp"]
                        if sp >= 2:
                            t.wait_ge(sems["ex"], sp - 1)
                        t.matmul(ps_s[:, sp % 2, :],
                                 k_r[rb:rb + 64, g, t2c * 128:(t2c + 1) * 128],
                                 q_r[rb:rb + 64, g, t10:t10 + 512],
                                 start=True, stop=False, skip_group_check=True)
                        mm = t.matmul(ps_s[:, sp % 2, :],
                                      ctx.consts["negones_row_f32"][0:1, :],
                                      crow[0:1, t10:t10 + 512],
                                      start=False, stop=True, skip_group_check=True)
                        mm.then_inc(sems["sp"], 1)
                        cnts["sp"] += 1
                        av = cnts["av"]
                        t.wait_ge(sems["ex"], av + 1)
                        if t2c == 0 and idx >= 2:
                            t.wait_ge(sems["attw"], idx - 1)
                        mm = t.matmul(ps_av[:, idx % 2, :][0:65, :],
                                      vt_h[:, t2c, :], a_db[:, av % 2, :],
                                      start=(t2c == 0), stop=(t2c == 15),
                                      skip_group_check=True)
                        mm.then_inc(sems["av"], 1)
                        cnts["av"] += 1
                    if idx >= 1:
                        t.wait_ge(sems["attw"], idx)
                    t.wait_ge(sems["nr"], 4 * idx + 1)
                    t.matmul(ps_rb[0:64, :], ctx.consts["ones_sq"][64:65, 0:64],
                             rrow[64:65, :], start=True, stop=True,
                             skip_group_check=True).then_inc(sems["nr"], 1)
                    if odd:
                        t.wait_ge(sems["nr"], 4 * idx + 4)
                        t.matmul(ps_rb[64:128, :], ctx.consts["ident128"][0:64, 0:64],
                                 tmp0f[0:64, :], start=True, stop=True,
                                 skip_group_check=True).then_inc(sems["mv"], 1)
                    cnts["nr"] += 1
            @b.scalar
            def _(s):
                for t1b in range(4):
                    idx = cnts["se"]
                    for t2c in range(16):
                        ex = cnts["ex"]
                        s.wait_ge(sems["sp"], ex + 1)
                        if ex >= 2:
                            s.wait_ge(sems["av"], ex - 1)
                        s.activation(out=a_db[:, ex % 2, :], in_=ps_s[:, ex % 2, :],
                                     func=AF.Exp, scale=0.125).then_inc(sems["ex"], 1)
                        cnts["ex"] += 1
                    s.wait_ge(sems["nr"], 4 * idx + 2)
                    s.activation(out=rb_s[:, :], in_=ps_rb[0:64, :],
                                 func=AF.Copy).then_inc(sems["nr"], 1)
                    cnts["se"] += 1
            @b.vector
            def _(v):
                for t1b in range(4):
                    t10 = t1b * 512
                    idx = cnts["vv"]
                    v.wait_ge(sems["av"], 16 * (idx + 1))
                    sl = idx % 2
                    v.reciprocal(out=rrow[64:65, :],
                                 in_=ps_av[:, sl, :][64:65, :]).then_inc(sems["nr"], 1)
                    v.wait_ge(sems["nr"], 4 * idx + 3)
                    if not odd:
                        v.tensor_mul(out=att[0:64, g, t10:t10 + 512],
                                     in0=ps_av[:, sl, :][0:64, :],
                                     in1=rb_s[0:64, :]).then_inc(sems["attw"], 1)
                        v.sem_inc(sems["nr"], 1)
                    else:
                        v.tensor_mul(out=tmp0f[0:64, :],
                                     in0=ps_av[:, sl, :][0:64, :],
                                     in1=rb_s[0:64, :]).then_inc(sems["nr"], 1)
                        v.wait_ge(sems["mv"], cnts["mv"] + 1)
                        cnts["mv"] += 1
                        v.tensor_copy(out=att[64:128, g, t10:t10 + 512],
                                      in_=ps_rb[64:128, :]).then_inc(sems["attw"], 1)
                    cnts["vv"] += 1


def _ln_stats_rows(ctx, XB, mu, r, eps):
    nc = ctx.nc
    with contextlib.ExitStack() as stack:
        x2db = ctx.sbuf([128, 2, 512], BF16, stack=stack, base="lns_x2")
        ms = ctx.sbuf([1, T], F32, stack=stack, base="lns_ms")
        es = ESync(ctx, stack)
        with contextlib.ExitStack() as pstack:
            ps_sx = ctx.psum([1, T], F32, stack=pstack, base="lns_psx")
            ps_sx2 = ctx.psum([1, T], F32, stack=pstack, base="lns_psx2")
            sem_sq = ctx.sem(pstack)
            sem_mm = ctx.sem(pstack)
            with nc.Block() as b:
                @b.scalar
                def _(s):
                    kk = 0
                    for blk in range(4):
                        for c in range(NCH):
                            if kk >= 2:
                                s.wait_ge(sem_mm, kk - 1)
                            s.activation(out=x2db[:, kk % 2, :],
                                         in_=XB[:, c, blk * 512:(blk + 1) * 512],
                                         func=AF.Square).then_inc(sem_sq, 1)
                            kk += 1
                @b.tensor
                def _(t):
                    oc = ctx.consts["ones_col_bf"]
                    kk = 0
                    for blk in range(4):
                        f0 = blk * 512
                        for c in range(NCH):
                            t.wait_ge(sem_sq, kk + 1)
                            t.matmul(ps_sx[0:1, f0:f0 + 512], oc[:, 0:1],
                                     XB[:, c, f0:f0 + 512],
                                     start=(c == 0), stop=(c == NCH - 1),
                                     skip_group_check=True)
                            mm = t.matmul(ps_sx2[0:1, f0:f0 + 512], oc[:, 0:1],
                                          x2db[:, kk % 2, :],
                                          start=(c == 0), stop=(c == NCH - 1),
                                          skip_group_check=True)
                            mm.then_inc(sem_mm, 1)
                            kk += 1
            with nc.Block() as b:
                @b.vector
                def _(v):
                    v.tensor_scalar_mul(out=mu[:], in0=ps_sx[0:1, :], scalar1=1.0 / D)
                    v.tensor_scalar_mul(out=ms[:], in0=ps_sx2[0:1, :], scalar1=1.0 / D)
        with nc.Block() as b:
            @b.vector
            def _(v):
                i = v.tensor_mul(out=r[:], in0=mu[:], in1=mu[:])
                es.edge(v, i)
                v.tensor_sub(out=ms[:], in0=ms[:], in1=r[:])
        with nc.Block() as b:
            @b.scalar
            def _(s):
                i = s.activation(out=ms[:], in_=ms[:], func=AF.Ln, bias=float(eps))
                es.edge(s, i)
                s.activation(out=r[:], in_=ms[:], func=AF.Exp, scale=-0.5)





def l0_attn_host_arrays(Wq, bq, Wk, bk, Wv, bv, Wg, bg, Wo, bo):
    def rearr(wT):
        K, M = wT.shape
        return np.ascontiguousarray(wT.reshape(K // 128, 128, M).transpose(1, 0, 2))
    mask = np.zeros((128, 128), np.float32)
    for a in range(128):
        mask[a, a + 1:] = -1e30
    return {
        "l0a_wq": rearr(Wq), "l0a_wk": rearr(Wk), "l0a_wv": rearr(Wv),
        "l0a_wg": rearr(Wg), "l0a_wo": rearr(Wo),
        "l0a_bq": bq.reshape(1, -1), "l0a_bk": bk.reshape(1, -1),
        "l0a_bv": bv.reshape(1, -1), "l0a_bg": bg.reshape(1, -1),
        "l0a_bo": bo.reshape(1, -1),
        "l0a_mask": mask,
    }


L0_ATTN_CONST_DTYPES = {"l0a_bq": BF16, "l0a_bk": BF16, "l0a_bv": BF16,
                        "l0a_bg": BF16, "l0a_bo": BF16}
L0_ATTN_CONSTS = ["l0a_bq", "l0a_bk", "l0a_bv", "l0a_bg", "l0a_bo", "l0a_mask"]


def emit_l0_attn(ctx, X, XLN):
    nc = ctx.nc
    D2 = 2048
    scr = {}
    for nm in ("q", "k", "g"):
        scr[nm] = nc.dram_tensor(ctx.name(f"l0a_{nm}scr"), [128, 16, T], BF16,
                                 kind="Internal").ap()
    scr["v"] = nc.dram_tensor(ctx.name("l0a_vscr"), [128, 16, D2], BF16,
                              kind="Internal").ap()
    ctx.dbg_l0a_scr = scr
    for (nm, wname, bname, func) in (("q", "l0a_wq", "l0a_bq", AF.Copy),
                                     ("k", "l0a_wk", "l0a_bk", AF.Copy),
                                     ("g", "l0a_wg", "l0a_bg", AF.Sigmoid)):
        with contextlib.ExitStack() as ws:
            wc = _load_const_scoped(ctx, ws, wname, BF16)
            brow = ctx.consts[bname]
            stage = ctx.sbuf([128, 2, 512], BF16, stack=ws, base="l0a_stg")
            ps = ctx.psum([128, 2, 512], F32, stack=ws, base="l0a_ps1")
            stt = SlotSems(ctx, ws, 2, kind="sw")
            sem_mm = ctx.sem(ws)
            sem_sc = ctx.sem(ws)
            with nc.Block() as b:
                @b.tensor
                def _(t):
                    for nn in range(64):
                        tb, oc = nn // 16, nn % 16
                        t0 = tb * 512
                        if nn >= 2:
                            t.wait_ge(sem_sc, nn - 1)
                        for dc in range(NCH):
                            t.matmul(ps[:, nn % 2, :], wc[:, dc, oc * 128:(oc + 1) * 128],
                                     XLN[:, dc, t0:t0 + 512],
                                     start=(dc == 0), stop=False, skip_group_check=True)
                        mm = t.matmul(ps[:, nn % 2, :], brow[0:1, oc * 128:(oc + 1) * 128],
                                      ctx.consts["ones_row512"][0:1, :],
                                      start=False, stop=True, skip_group_check=True)
                        mm.then_inc(sem_mm, 1)
                @b.scalar
                def _(s):
                    for nn in range(64):
                        s.wait_ge(sem_mm, nn + 1)
                        if nn >= 2:
                            stt.wait(s, nn % 2, 16 * ((nn - 2) // 2 + 1))
                        s.activation(out=stage[:, nn % 2, :], in_=ps[:, nn % 2, :],
                                     func=func).then_inc(sem_sc, 1)
                @b.gpsimd
                def _(g_):
                    for nn in range(64):
                        tb, oc = nn // 16, nn % 16
                        t0 = tb * 512
                        g_.wait_ge(sem_sc, nn + 1)
                        stt.inc(g_.dma_start(out=scr[nm][:, oc, t0:t0 + 512],
                                             in_=stage[:, nn % 2, :]), nn % 2)
            stt.barrier(0, 16 * 32)
            stt.barrier(1, 16 * 32)
    with contextlib.ExitStack() as ws:
        wc = _load_const_scoped(ctx, ws, "l0a_wv", BF16)
        brow = ctx.consts["l0a_bv"]
        stage = ctx.sbuf([128, 2, 512], BF16, stack=ws, base="l0a_stgv")
        ps = ctx.psum([128, 2, 512], F32, stack=ws, base="l0a_ps1v")
        stt = SlotSems(ctx, ws, 2, kind="sw")
        sem_mm = ctx.sem(ws)
        sem_sc = ctx.sem(ws)
        with nc.Block() as b:
            @b.tensor
            def _(t):
                for nn in range(64):
                    tc, eh = nn // 4, nn % 4
                    e0 = eh * 512
                    if nn >= 2:
                        t.wait_ge(sem_sc, nn - 1)
                    for dc in range(NCH):
                        t.matmul(ps[:, nn % 2, :], XLN[:, dc, tc * 128:(tc + 1) * 128],
                                 wc[:, dc, e0:e0 + 512],
                                 start=(dc == 0), stop=False, skip_group_check=True)
                    mm = t.matmul(ps[:, nn % 2, :], ctx.consts["ones_row_bf"][0:1, :],
                                  brow[0:1, e0:e0 + 512],
                                  start=False, stop=True, skip_group_check=True)
                    mm.then_inc(sem_mm, 1)
            @b.scalar
            def _(s):
                for nn in range(64):
                    s.wait_ge(sem_mm, nn + 1)
                    if nn >= 2:
                        stt.wait(s, nn % 2, 16 * ((nn - 2) // 2 + 1))
                    s.activation(out=stage[:, nn % 2, :], in_=ps[:, nn % 2, :],
                                 func=AF.Copy).then_inc(sem_sc, 1)
            @b.gpsimd
            def _(g_):
                for nn in range(64):
                    tc, eh = nn // 4, nn % 4
                    e0 = eh * 512
                    g_.wait_ge(sem_sc, nn + 1)
                    stt.inc(g_.dma_start(out=scr["v"][:, tc, e0:e0 + 512],
                                         in_=stage[:, nn % 2, :]), nn % 2)
        stt.barrier(0, 16 * 32)
        stt.barrier(1, 16 * 32)
    with contextlib.ExitStack() as p2:
        wo = _load_const_scoped(ctx, p2, "l0a_wo", BF16)
        mask = ctx.consts["l0a_mask"]
        borow = ctx.consts["l0a_bo"]
        qi = ctx.sbuf([128, 16, 128], BF16, stack=p2, base="l0a_qi")
        gi = ctx.sbuf([128, 16, 128], BF16, stack=p2, base="l0a_gi")
        kloc = ctx.sbuf([128, 16, 128], BF16, stack=p2, base="l0a_kloc")
        vloc = ctx.sbuf([128, 16, 128], BF16, stack=p2, base="l0a_vloc")
        kj = ctx.sbuf([128, 2, 16, 128], BF16, stack=p2, base="l0a_kj")
        vj = ctx.sbuf([128, 2, 16, 128], BF16, stack=p2, base="l0a_vj")
        abuf = ctx.sbuf([128, 2, 128], BF16, stack=p2, base="l0a_ab")
        sml = ctx.sbuf([128, 128], F32, stack=p2, base="l0a_sml")
        nmx = ctx.sbuf([128, 3], F32, stack=p2, base="l0a_nmx")
        al32 = ctx.sbuf([128, 128], F32, stack=p2, base="l0a_al32")
        alocal = ctx.sbuf([128, 128], BF16, stack=p2, base="l0a_al")
        hprime = ctx.sbuf([128, 16, 128], BF16, stack=p2, base="l0a_hp")
        scale = 1.0 / float(np.sqrt(D2))
        for i in range(16):
            with contextlib.ExitStack() as pi:
                sem = {k: ctx.sem(pi) for k in ("sp", "ex", "av", "lt")}
                dld = ctx.dsem(pi, "sw")
                kvld = SlotSems(ctx, pi, 2, kind="sw")
                es = ESync(ctx, pi)
                ps_out = ctx.psum([128, 16, 128], F32, stack=pi, base="l0a_psout")
                with nc.Block() as b:
                    @b.tensor
                    def _(t):
                        for j4 in range(4):
                            t.matmul(ps_out[:, j4 * 4:(j4 + 1) * 4, :],
                                     ctx.consts["ones_row_bf"][0:1, :],
                                     ctx.consts["zeros_row512"][0:1, :],
                                     start=True, stop=False, skip_group_check=True)
                with nc.Block() as b:
                    @b.gpsimd
                    def _(g_):
                        dld.inc(g_.dma_start(out=qi[:], in_=scr["q"][:, :, i * 128:(i + 1) * 128]))
                        dld.inc(g_.dma_start(out=gi[:], in_=scr["g"][:, :, i * 128:(i + 1) * 128]))
                        dld.inc(g_.dma_start(out=kloc[:], in_=scr["k"][:, :, i * 128:(i + 1) * 128]))
                        dld.inc(g_.dma_start(out=vloc[:], in_=scr["v"][:, i, :].rearrange("p (a b) -> p a b", b=128)))
                dld.barrier(64)
                lstack = pi.enter_context(contextlib.ExitStack())
                ps_lt = ctx.psum([128, 128], F32, stack=lstack, base="l0a_pslt")
                ps_t = ctx.psum([128, 128], BF16, stack=lstack, base="l0a_pst")
                with nc.Block() as b:
                    @b.tensor
                    def _(t):
                        for kc in range(16):
                            t.matmul(ps_lt[:, :], qi[:, kc, :], kloc[:, kc, :],
                                     start=(kc == 0), stop=(kc == 15))
                with nc.Block() as b:
                    @b.vector
                    def _(v):
                        i1 = v.tensor_add(out=sml[:], in0=ps_lt[:], in1=mask[:, :])
                        es.edge(v, i1)
                        i1 = v.reduce_max(out=nmx[:, 0:1], in_=sml[:], axis=AX.X,
                                          negate=True)
                        es.edge(v, i1)
                        v.tensor_scalar_mul(out=nmx[:, 1:2], in0=nmx[:, 0:1],
                                            scalar1=float(scale))
                with nc.Block() as b:
                    @b.scalar
                    def _(s):
                        s.activation(out=al32[:], in_=sml[:], func=AF.Exp,
                                     scale=float(scale), bias=nmx[:, 1:2],
                                     accum_out=nmx[:, 2:3])
                with nc.Block() as b:
                    @b.vector
                    def _(v):
                        i1 = v.reciprocal(out=nmx[:, 2:3], in_=nmx[:, 2:3])
                        es.edge(v, i1)
                        v.tensor_scalar_mul(out=alocal[:], in0=al32[:],
                                            scalar1=nmx[:, 2:3])
                with nc.Block() as b:
                    @b.tensor
                    def _(t):
                        t.transpose(ps_t[:, :], alocal[:, :], ctx.consts["ident128_bf"][:, :])
                with nc.Block() as b:
                    @b.scalar
                    def _(s):
                        s.activation(out=alocal[:], in_=ps_t[:], func=AF.Copy)
                lstack.close()
                ps_sp = ctx.psum([128, 2, 512], F32, stack=pi, base="l0a_pssp")
                with nc.Block() as b:
                    @b.gpsimd
                    def _(g_):
                        for j in range(i):
                            if j >= 2:
                                g_.wait_ge(sem["av"], (j - 2) * 16 + 16)
                            kvld.inc(g_.dma_start(out=kj[:, j % 2, :, :],
                                                  in_=scr["k"][:, :, j * 128:(j + 1) * 128]), j % 2)
                            kvld.inc(g_.dma_start(out=vj[:, j % 2, :, :],
                                                  in_=scr["v"][:, j, :].rearrange("p (a b) -> p a b", b=128)), j % 2)
                    @b.tensor
                    def _(t):
                        for j in range(i):
                            kvld.wait(t, j % 2, 32 * (j // 2 + 1))
                            if j >= 2:
                                t.wait_ge(sem["ex"], j - 1)
                            for kc in range(16):
                                mm = t.matmul(ps_sp[:, j % 2, 0:128], kj[:, j % 2, kc, :],
                                              qi[:, kc, :], start=(kc == 0), stop=(kc == 15),
                                              skip_group_check=True)
                            mm.then_inc(sem["sp"], 1)
                            t.wait_ge(sem["ex"], j + 1)
                            for ec in range(16):
                                mm = t.matmul(ps_out[:, ec, :], vj[:, j % 2, ec, :],
                                              abuf[:, j % 2, :], start=False, stop=False,
                                              skip_group_check=True)
                            mm.then_inc(sem["av"], 16)
                        for ec in range(16):
                            t.matmul(ps_out[:, ec, :], vloc[:, ec, :], alocal[:, :],
                                     start=False, stop=True, skip_group_check=True)
                    @b.scalar
                    def _(s):
                        for j in range(i):
                            s.wait_ge(sem["sp"], j + 1)
                            if j >= 2:
                                s.wait_ge(sem["av"], (j - 2) * 16 + 16)
                            s.activation(out=abuf[:, j % 2, :], in_=ps_sp[:, j % 2, 0:128],
                                         func=AF.Copy).then_inc(sem["ex"], 1)
                with nc.Block() as b:
                    @b.vector
                    def _(v):
                        for ec in range(16):
                            i1 = v.tensor_mul(out=hprime[:, ec, :], in0=ps_out[:, ec, :],
                                              in1=gi[:, ec, :])
                        es.edge(v, i1)
                with contextlib.ExitStack() as px:
                    ps_x = ctx.psum([128, 8, 128], F32, stack=px, base="l0a_psx")
                    with nc.Block() as b:
                        @b.tensor
                        def _(t):
                            for oc in range(NCH):
                                for kc in range(16):
                                    t.matmul(ps_x[:, oc, :], wo[:, kc, oc * 128:(oc + 1) * 128],
                                             hprime[:, kc, :], start=(kc == 0), stop=False,
                                             skip_group_check=True)
                                t.matmul(ps_x[:, oc, :], borow[0:1, oc * 128:(oc + 1) * 128],
                                         ctx.consts["ones_row512"][0:1, 0:128],
                                         start=False, stop=True, skip_group_check=True)
                    with nc.Block() as b:
                        @b.vector
                        def _(v):
                            for oc in range(NCH):
                                i1 = v.tensor_add(out=X[:, oc, i * 128:(i + 1) * 128],
                                                  in0=X[:, oc, i * 128:(i + 1) * 128],
                                                  in1=ps_x[:, oc, :])
                            es.edge(v, i1)



def l1_attn_host_arrays(w_in, b_in, A, Bm, Cm, Dv, w_out, b_out, g_norm):
    def sc_layout(M):
        out = np.zeros((128, 512), np.float32)
        for dh in range(2):
            out[dh * 64:(dh + 1) * 64, :] = M[dh * 512:(dh + 1) * 512, :].T
        return out
    sel = np.zeros((8, 4, 128), np.float32)
    for j in range(4):
        sel[2 * j, j, 0:64] = 1.0
        sel[2 * j + 1, j, 64:128] = 1.0
    def rearr(wT):
        K, M = wT.shape
        return np.ascontiguousarray(wT.reshape(K // 128, 128, M).transpose(1, 0, 2))
    return {
        "l1a_winT": rearr(np.ascontiguousarray(w_in.T)),
        "l1a_bin": b_in.reshape(1, -1),
        "l1a_A": sc_layout(A), "l1a_Bm": sc_layout(Bm),
        "l1a_CmT": np.ascontiguousarray(Cm.T),
        "l1a_dvc": mk_col8(Dv),
        "l1a_woutT": rearr(np.ascontiguousarray(w_out.T)),
        "l1a_boc": mk_col8(b_out),
        "l1a_gnc": mk_col8(g_norm),
        "l1a_sel": np.ascontiguousarray(sel.reshape(8, 4 * 128)),
    }


L1_ATTN_CONST_DTYPES = {"l1a_winT": BF16, "l1a_bin": BF16, "l1a_woutT": BF16,
                        "l1a_sel": BF16}
L1_ATTN_CONSTS = ["l1a_A", "l1a_Bm", "l1a_CmT", "l1a_dvc", "l1a_boc",
                  "l1a_gnc", "l1a_sel"]
TP = 1024


def emit_l1_attn(ctx, X, XLN):
    nc = ctx.nc
    xp_dram = nc.dram_tensor(ctx.name("l1a_xpd"), [128, 8, TP], BF16, kind="Internal").ap()
    ctx.dbg_l1 = {"xp_dram": xp_dram}
    with contextlib.ExitStack() as stack:
        xpf = ctx.sbuf([128, NCH, TP], BF16, stack=stack, base="l1a_xpf")
        gsf = ctx.sbuf([128, NCH, TP], BF16, stack=stack, base="l1a_gsf")
        ysf = ctx.sbuf([128, NCH, TP], BF16, stack=stack, base="l1a_ysf")
        esv = ESync(ctx, stack)
        xs_scope = stack.enter_context(contextlib.ExitStack())
        xs = ctx.sbuf([128, NCH, TP], BF16, stack=xs_scope, base="l1a_xs")
        with contextlib.ExitStack() as rs:
            x2db = ctx.sbuf([128, 2, TP], BF16, stack=rs, base="l1a_x2")
            msrow = ctx.sbuf([1, TP], F32, stack=rs, base="l1a_ms")
            sem_sq = ctx.sem(rs)
            sem_mm = ctx.sem(rs)
            es = ESync(ctx, rs)
            with contextlib.ExitStack() as ps_:
                ps_ms = ctx.psum([1, TP], F32, stack=ps_, base="l1a_psms")
                with nc.Block() as b:
                    @b.scalar
                    def _(s):
                        for c in range(NCH):
                            if c >= 2:
                                s.wait_ge(sem_mm, c - 1)
                            s.activation(out=x2db[:, c % 2, :], in_=XLN[:, c, 0:T:2],
                                         func=AF.Square).then_inc(sem_sq, 1)
                    @b.tensor
                    def _(t):
                        oc = ctx.consts["ones_col_bf"]
                        for c in range(NCH):
                            t.wait_ge(sem_sq, c + 1)
                            for half in range(2):
                                mm = t.matmul(ps_ms[0:1, half * 512:(half + 1) * 512],
                                              oc[:, 0:1], x2db[:, c % 2, half * 512:(half + 1) * 512],
                                              start=(c == 0), stop=(c == NCH - 1),
                                              skip_group_check=True)
                            mm.then_inc(sem_mm, 1)
                with nc.Block() as b:
                    @b.vector
                    def _(v):
                        v.tensor_scalar_mul(out=msrow[:], in0=ps_ms[0:1, :], scalar1=1.0 / D)
            with nc.Block() as b:
                @b.scalar
                def _(s):
                    i = s.activation(out=msrow[:], in_=msrow[:], func=AF.Ln, bias=1e-6)
                    es.edge(s, i)
                    s.activation(out=msrow[:], in_=msrow[:], func=AF.Exp, scale=-0.5)
            with contextlib.ExitStack() as ps_:
                ps_rb = ctx.psum([128, TP], F32, stack=ps_, base="l1a_psrb")
                gnc = ctx.consts["l1a_gnc"]
                with nc.Block() as b:
                    @b.tensor
                    def _(t):
                        for half in range(2):
                            t.matmul(ps_rb[:, half * 512:(half + 1) * 512],
                                     ctx.consts["ones_row_f32"][0:1, :],
                                     msrow[0:1, half * 512:(half + 1) * 512],
                                     start=True, stop=True)
                with nc.Block() as b:
                    @b.vector
                    def _(v):
                        for c in range(NCH):
                            i = v.tensor_mul(out=x2db[:, 0, :], in0=XLN[:, c, 0:T:2],
                                             in1=ps_rb[:, :])
                            es.edge(v, i)
                            i = v.tensor_scalar(out=xs[:, c, :], in0=x2db[:, 0, :],
                                                scalar1=gnc[:, c:c + 1], scalar2=None,
                                                op0=ALU.mult)
                            es.edge(v, i)
        for half_w in range(2):
            with contextlib.ExitStack() as pj:
                wname = "l1a_winT"
                dr = ctx.dram[wname]
                winT = ctx.sbuf([128, NCH, 1024], BF16, stack=pj, base="l1a_win")
                binr = ctx.sbuf([1, 2048], BF16, stack=pj, base="l1a_bin")
                with contextlib.ExitStack() as ls:
                    dls = ctx.dsem(ls, "sw")
                    with nc.Block() as b:
                        @b.gpsimd
                        def _(g_):
                            dls.inc(g_.dma_start(out=winT[:],
                                                 in_=dr[:, :, half_w * 1024:(half_w + 1) * 1024]))
                            dls.inc(g_.dma_start(out=binr[:], in_=ctx.dram["l1a_bin"]))
                    dls.barrier(32)
                sem_mm = ctx.sem(pj)
                sem_ev = ctx.sem(pj)
                stg = ctx.sbuf([128, 2, 512], BF16, stack=pj, base="l1a_stg")
                stq = SlotSems(ctx, pj, 2, kind="sw")
                with contextlib.ExitStack() as ps_:
                    ps = ctx.psum([128, 2, 512], F32, stack=ps_, base="l1a_psp")
                    ntok = 16 if half_w == 0 else 0
                    with nc.Block() as b:
                        @b.tensor
                        def _(t):
                            nn = 0
                            for ec in range(8):
                                for tb in range(2):
                                    if nn >= 2:
                                        t.wait_ge(sem_ev, nn - 1)
                                    for dc in range(NCH):
                                        t.matmul(ps[:, nn % 2, :],
                                                 winT[:, dc, ec * 128:(ec + 1) * 128],
                                                 xs[:, dc, tb * 512:(tb + 1) * 512],
                                                 start=(dc == 0), stop=False,
                                                 skip_group_check=True)
                                    mm = t.matmul(ps[:, nn % 2, :],
                                                  binr[0:1, half_w * 1024 + ec * 128:half_w * 1024 + (ec + 1) * 128],
                                                  ctx.consts["ones_row512"][0:1, :],
                                                  start=False, stop=True, skip_group_check=True)
                                    mm.then_inc(sem_mm, 1)
                                    nn += 1
                            for tc in range(NCH if ntok else 0):
                                for eb in range(2):
                                    if nn >= 2:
                                        t.wait_ge(sem_ev, nn - 1)
                                    for dc in range(NCH):
                                        t.matmul(ps[:, nn % 2, :],
                                                 xs[:, dc, tc * 128:(tc + 1) * 128],
                                                 winT[:, dc, eb * 512:(eb + 1) * 512],
                                                 start=(dc == 0), stop=False,
                                                 skip_group_check=True)
                                    mm = t.matmul(ps[:, nn % 2, :],
                                                  ctx.consts["ones_row_bf"][0:1, :],
                                                  binr[0:1, eb * 512:(eb + 1) * 512],
                                                  start=False, stop=True, skip_group_check=True)
                                    mm.then_inc(sem_mm, 1)
                                    nn += 1
                        @b.scalar
                        def _(s):
                            nn = 0
                            for ec in range(8):
                                for tb in range(2):
                                    s.wait_ge(sem_mm, nn + 1)
                                    func = AF.Copy if half_w == 0 else AF.Silu
                                    dst = (xpf if half_w == 0 else gsf)[:, ec, tb * 512:(tb + 1) * 512]
                                    s.activation(out=dst, in_=ps[:, nn % 2, :],
                                                 func=func).then_inc(sem_ev, 1)
                                    nn += 1
                            for tc in range(NCH if ntok else 0):
                                for eb in range(2):
                                    s.wait_ge(sem_mm, nn + 1)
                                    if nn >= 16 + 2:
                                        stq.wait(s, nn % 2, 16 * ((nn - 16 - 2) // 2 + 1))
                                    s.activation(out=stg[:, nn % 2, :], in_=ps[:, nn % 2, :],
                                                 func=AF.Copy).then_inc(sem_ev, 1)
                                    nn += 1
                        @b.gpsimd
                        def _(g_):
                            nn = 16
                            for tc in range(NCH if ntok else 0):
                                for eb in range(2):
                                    g_.wait_ge(sem_ev, nn + 1)
                                    stq.inc(g_.dma_start(
                                        out=xp_dram[:, tc, eb * 512:(eb + 1) * 512],
                                        in_=stg[:, nn % 2, :]), nn % 2)
                                    nn += 1
                if half_w == 0:
                    stq.barrier(0, 16 * 8)
                    stq.barrier(1, 16 * 8)
        ctx.dbg_l1["bufs"] = dict(xpf=xpf, gsf=gsf, ysf=ysf)
        xs_scope.close()
        _l1_scan(ctx, stack, xp_dram, ysf, esv)
        with contextlib.ExitStack() as fz:
            woT = _load_const_scoped(ctx, fz, "l1a_woutT", BF16)
            dvc = ctx.consts["l1a_dvc"]
            boc = ctx.consts["l1a_boc"]
            yg = ctx.sbuf([128, NCH, TP], BF16, stack=fz, base="l1a_yg")
            tmpv = ctx.sbuf([128, 2, TP], F32, stack=fz, base="l1a_tv")
            es = ESync(ctx, fz)
            with nc.Block() as b:
                @b.vector
                def _(v):
                    for c in range(NCH):
                        i = v.tensor_scalar(out=tmpv[:, c % 2, :], in0=xpf[:, c, :],
                                            scalar1=dvc[:, c:c + 1], scalar2=None,
                                            op0=ALU.mult)
                        es.edge(v, i)
                        i = v.tensor_add(out=tmpv[:, c % 2, :], in0=tmpv[:, c % 2, :],
                                         in1=ysf[:, c, :])
                        es.edge(v, i)
                        i = v.tensor_mul(out=yg[:, c, :], in0=tmpv[:, c % 2, :],
                                         in1=gsf[:, c, :])
                        es.edge(v, i)
            with contextlib.ExitStack() as ps_:
                ps_o = ctx.psum([128, 2, 512], F32, stack=ps_, base="l1a_pso")
                sem_om = ctx.sem(ps_)
                sem_oe = ctx.sem(ps_)
                with nc.Block() as b:
                    @b.tensor
                    def _(t):
                        nn = 0
                        for oc in range(NCH):
                            for tb in range(2):
                                if nn >= 2:
                                    t.wait_ge(sem_oe, nn - 1)
                                for dc in range(NCH):
                                    mm = t.matmul(ps_o[:, nn % 2, :],
                                                  woT[:, dc, oc * 128:(oc + 1) * 128],
                                                  yg[:, dc, tb * 512:(tb + 1) * 512],
                                                  start=(dc == 0), stop=(dc == NCH - 1),
                                                  skip_group_check=True)
                                mm.then_inc(sem_om, 1)
                                nn += 1
                    @b.vector
                    def _(v):
                        nn = 0
                        for oc in range(NCH):
                            for tb in range(2):
                                v.wait_ge(sem_om, nn + 1)
                                for par in range(2):
                                    i = v.scalar_tensor_tensor(
                                        out=X[:, oc, tb * 1024 + par:(tb + 1) * 1024:2],
                                        in0=ps_o[:, nn % 2, :], scalar=boc[:, oc:oc + 1],
                                        in1=X[:, oc, tb * 1024 + par:(tb + 1) * 1024:2],
                                        op0=ALU.add, op1=ALU.add)
                                i.then_inc(sem_oe, 1)
                                nn += 1


def _l1_scan(ctx, stack, xp_dram, ysf, esv):
    nc = ctx.nc
    NW = TP // 64
    with contextlib.ExitStack() as sc:
        Ac = _load_const_scoped(ctx, sc, "l1a_A", F32)
        Bc = _load_const_scoped(ctx, sc, "l1a_Bm", F32)
        selc = _load_const_scoped(ctx, sc, "l1a_sel", BF16)
        hbuf = ctx.sbuf([128, 2, 512], F32, stack=sc, base="l1a_h")
        zbuf = ctx.sbuf([128, 2, 512], F32, stack=sc, base="l1a_z")
        zabuf = ctx.sbuf([128, 2, 512], F32, stack=sc, base="l1a_za")
        ccol = ctx.sbuf([128, 1], F32, stack=sc, base="l1a_cc")
        wld = SlotSems(ctx, sc, 2, kind="sw")
        cld = ctx.dsem(sc, "sw")
        sem_xtb = ctx.sem(sc)
        sem_z = ctx.sem(sc)
        sem_h = ctx.sem(sc)
        sem_y = ctx.sem(sc)
        sem_ye = ctx.sem(sc)
        with contextlib.ExitStack() as cs_:
            cmt = _load_const_scoped(ctx, cs_, "l1a_CmT", F32)
            ctmp = ctx.sbuf([64, 1], F32, stack=cs_, base="l1a_ct")
            with nc.Block() as b:
                @b.vector
                def _(v):
                    v.tensor_reduce(out=ctmp[:], in_=cmt[:, :], axis=AX.X, op=ALU.add)
            with nc.Block() as b:
                @b.gpsimd
                def _(g_):
                    cld.inc(g_.dma_start(out=ccol[0:64, :], in_=ctmp[:, :]))
                    cld.inc(g_.dma_start(out=ccol[64:128, :], in_=ctmp[:, :]))
            cld.barrier(32)
        xw = ctx.sbuf([128, 2, 16, 512], BF16, stack=sc, base="l1a_xw")
        with nc.Block() as b:
            @b.vector
            def _(v):
                v.memset(hbuf[:, 1, :], 0.0)
        SEG = 128
        with contextlib.ExitStack() as ps_:
            ps_xtb = ctx.psum([128, 2, 512], F32, stack=ps_, base="l1a_psxtb")
            ps_y = ctx.psum([128, 2, 8, SEG], F32, stack=ps_, base="l1a_psy")
            with nc.Block() as b:
                @b.gpsimd
                def _(g_):
                    for w in range(NW):
                        if w >= 2:
                            g_.wait_ge(sem_xtb, (w - 1) * 64)
                        p0 = (w * 64) % 128
                        tcw = w // 2
                        for j in range(4):
                            src2 = xp_dram[p0 + 16 * j:p0 + 16 * j + 16, tcw:tcw + 1, :]
                            wld.inc(g_.dma_start(
                                out=xw[2 * j:2 * j + 2, w % 2, :, :],
                                in_=src2.rearrange("l one (h d) -> (one h) l d", h=2)), w % 2)
                @b.tensor
                def _(t):
                    def y_mms(s):
                        seg = s // SEG
                        sm = s % SEG
                        if sm == 0 and seg >= 2:
                            t.wait_ge(sem_ye, seg - 1)
                        for dc in range(8):
                            half, f0 = dc // 4, (dc % 4) * 128
                            mm = t.matmul(ps_y[:, seg % 2, dc, sm:sm + 1],
                                          hbuf[half * 64:(half + 1) * 64, s % 2, f0:f0 + 128],
                                          ccol[half * 64:(half + 1) * 64, 0:1],
                                          start=True, stop=True, skip_group_check=True)
                        mm.then_inc(sem_y, 1)
                    for s in range(TP):
                        w, l = s // 64, s % 64
                        j, slot = l // 16, l % 16
                        if l == 0:
                            wld.wait(t, w % 2, 16 * 4 * (w // 2 + 1))
                        if s >= 2:
                            t.wait_ge(sem_z, s - 1)
                        t.matmul(ps_xtb[:, s % 2, :], selc[0:8, j * 128:(j + 1) * 128],
                                 xw[0:8, w % 2, slot, :],
                                 start=True, stop=True,
                                 skip_group_check=True).then_inc(sem_xtb, 1)
                        if s >= 1:
                            t.wait_ge(sem_h, s)
                            y_mms(s - 1)
                    t.wait_ge(sem_h, TP)
                    y_mms(TP - 1)
                @b.vector
                def _(v):
                    for s in range(TP):
                        v.wait_ge(sem_xtb, s + 1)
                        i = v.tensor_mul(out=zabuf[:, s % 2, :],
                                         in0=ps_xtb[:, s % 2, :], in1=Bc[:, :])
                        if s >= 1:
                            v.wait_ge(sem_h, s)
                        i2 = v.tensor_mul(out=zbuf[:, s % 2, :],
                                          in0=hbuf[:, (s - 1) % 2, :], in1=Ac[:, :])
                        esv.edge(v, i2)
                        v.tensor_add(out=zbuf[:, s % 2, :], in0=zbuf[:, s % 2, :],
                                     in1=zabuf[:, s % 2, :]).then_inc(sem_z, 1)
                        if s % SEG == SEG - 1 and s >= SEG:
                            seg = s // SEG - 1
                            v.wait_ge(sem_y, (seg + 1) * SEG)
                            for dc in range(8):
                                i = v.tensor_copy(out=ysf[:, dc, seg * SEG:(seg + 1) * SEG],
                                                  in_=ps_y[:, seg % 2, dc, :])
                            i.then_inc(sem_ye, 1)
                @b.scalar
                def _(sE):
                    for s in range(TP):
                        sE.wait_ge(sem_z, s + 1)
                        if s >= 2:
                            sE.wait_ge(sem_y, s - 1)
                        sE.activation(out=hbuf[:, s % 2, :], in_=zbuf[:, s % 2, :],
                                      func=AF.Tanh).then_inc(sem_h, 1)
            for seg in (NW * 64 // SEG - 1,):
                with nc.Block() as b:
                    @b.vector
                    def _(v):
                        v.wait_ge(sem_y, min((seg + 1) * SEG, TP))
                        for dc in range(8):
                            i = v.tensor_copy(out=ysf[:, dc, seg * SEG:(seg + 1) * SEG],
                                              in_=ps_y[:, seg % 2, dc, :])
                        esv.edge(v, i)



def l2_attn_host_arrays(conv1_w, conv1_b, conv2_w, conv2_b, ln0_g, ln0_b,
                        Wq, bq, Wk, bk, Wv, bv, lnA_g, lnA_b,
                        W1, b1, W2, b2, ln2_g, ln2_b):
    c1t = np.zeros((32, 5, 128, 128), np.float32)
    for och in range(32):
        for tap in range(5):
            for m in range(128):
                oc = och * 128 + m
                k = oc // 4 - 128 * (och // 4)
                c1t[och, tap, k, m] = conv1_w[oc, 0, tap]
    c2t = np.zeros((8, 4, 5, 128, 128), np.float32)
    for oc2 in range(8):
        for ks in range(4):
            for tap in range(5):
                for m in range(128):
                    g = oc2 * 128 + m
                    for j in range(4):
                        r = 4 * g + j - 128 * (4 * oc2 + ks)
                        if 0 <= r < 128:
                            c2t[oc2, ks, tap, r, m] = conv2_w[g, j, tap]
    inv = 1.0 / 10000.0 ** (np.arange(0, 1024, 2, dtype=np.float32) / 1024)
    tt = np.arange(T, dtype=np.float32)
    fr = np.outer(tt, inv)
    cosf = np.cos(fr); sinf = np.sin(fr)
    C1 = np.zeros((128, 8, T), np.float32)
    C2 = np.zeros((128, 8, T), np.float32)
    for c in range(8):
        for r in range(128):
            d = c * 128 + r
            i = d // 2
            C1[r, c] = cosf[:, i]
            C2[r, c] = (-sinf[:, i]) if d % 2 == 0 else sinf[:, i]
    Ct = np.zeros((128, 16, 512), np.float32)
    St = np.zeros((128, 16, 512), np.float32)
    for tc in range(16):
        for p in range(128):
            Ct[p, tc] = cosf[tc * 128 + p]
            St[p, tc] = sinf[tc * 128 + p]
    pswap = np.zeros((128, 128), np.float32)
    for k in range(128):
        pswap[k, k ^ 1] = 1.0
    hmask = np.zeros((128, 2), np.float32)
    hmask[0:64, 0] = 1.0
    hmask[64:128, 1] = 1.0
    ind2 = np.zeros((2, 128), np.float32)
    ind2[0, 0:64] = 1.0
    ind2[1, 64:128] = 1.0
    def rearr(wT):
        K, M = wT.shape
        return np.ascontiguousarray(wT.reshape(K // 128, 128, M).transpose(1, 0, 2))
    return {
        "l2a_c1t": np.ascontiguousarray(c1t.reshape(32, 5 * 128, 128)),
        "l2a_c1b": conv1_b.reshape(1, -1),
        "l2a_c2t": np.ascontiguousarray(c2t.reshape(8, 4 * 5 * 128, 128)),
        "l2a_c2b": conv2_b.reshape(1, -1),
        "l2a_g0": mk_col8(ln0_g), "l2a_b0": mk_col8(ln0_b),
        "l2a_wq": np.ascontiguousarray(Wq.reshape(4, 8, 128, 1024).transpose(0, 2, 1, 3).reshape(4, 128, 8, 1024)),
        "l2a_bq": bq.reshape(4, 1, 1024),
        "l2a_wk": np.ascontiguousarray(Wk.reshape(4, 8, 128, 1024).transpose(0, 2, 1, 3).reshape(4, 128, 8, 1024)),
        "l2a_bk": bk.reshape(4, 1, 1024),
        "l2a_wv": np.ascontiguousarray(Wv.reshape(4, 8, 128, 1024).transpose(0, 2, 1, 3).reshape(4, 128, 8, 1024)),
        "l2a_bv": bv.reshape(4, 1, 1024),
        "l2a_gA": np.ascontiguousarray(lnA_g.reshape(4, 8, 128).transpose(0, 2, 1)),
        "l2a_bA": np.ascontiguousarray(lnA_b.reshape(4, 8, 128).transpose(0, 2, 1)),
        "l2a_w1T": rearr(np.ascontiguousarray(W1)),
        "l2a_b1c": np.ascontiguousarray(b1.reshape(32, 128).T),
        "l2a_w2T": rearr(np.ascontiguousarray(W2)),
        "l2a_b2c": mk_col8(b2),
        "l2a_g2": mk_col8(ln2_g), "l2a_b2": mk_col8(ln2_b),
        "l2a_C1": np.ascontiguousarray(C1.reshape(128, 8 * T)),
        "l2a_C2": np.ascontiguousarray(C2.reshape(128, 8 * T)),
        "l2a_Ct": np.ascontiguousarray(Ct.reshape(128, 16 * 512)),
        "l2a_St": np.ascontiguousarray(St.reshape(128, 16 * 512)),
        "l2a_pswap": pswap, "l2a_hmask": hmask, "l2a_ind2": ind2,
    }


L2_ATTN_CONST_DTYPES = {"l2a_c1t": BF16, "l2a_c1b": BF16, "l2a_c2t": BF16,
                        "l2a_c2b": BF16, "l2a_pswap": BF16,
                        "l2a_C1": BF16, "l2a_C2": BF16, "l2a_Ct": BF16,
                        "l2a_St": BF16}
L2_ATTN_CONSTS = ["l2a_g0", "l2a_b0", "l2a_g2", "l2a_b2", "l2a_hmask", "l2a_ind2", "l2a_pswap"]


def emit_l2_attn(ctx, X, XLN):
    nc = ctx.nc
    h_dram = nc.dram_tensor(ctx.name("l2a_hd"), [128, NCH, T], F32, kind="Internal").ap()
    t_dram = nc.dram_tensor(ctx.name("l2a_td"), [128, NCH, T], F32, kind="Internal").ap()
    _l2a_conv(ctx, X, XLN, t_dram)
    _l2a_stream_affine_ln(ctx, t_dram, h_dram, XLN, "l2a_g0", "l2a_b0", None, None)
    for it in range(4):
        _l2a_iter(ctx, XLN, h_dram, t_dram, it)
        gname = ("l2a_gA", it)
        _l2a_stream_affine_ln(ctx, t_dram, h_dram, XLN, "l2a_gA", "l2a_bA", it, None)
    _l2a_ffn(ctx, X, XLN, h_dram, t_dram)
    _l2a_stream_affine_ln(ctx, t_dram, None, None, "l2a_g2", "l2a_b2", None, X)


def _l2a_conv(ctx, X, XLN, t_dram):
    nc = ctx.nc
    NSPL = ((0, 256), (256, 512), (512, 516))
    with contextlib.ExitStack() as stack:
        up = ctx.sbuf([128, NCH, 2056], BF16, stack=stack, base="l2c_up")
        h1 = ctx.sbuf([128, 32, 520], BF16, stack=stack, base="l2c_h1")
        c1b = _load_const_scoped(ctx, stack, "l2a_c1b", BF16)
        c2b = _load_const_scoped(ctx, stack, "l2a_c2b", BF16)
        tile1 = ctx.sbuf([128, 2, 5 * 128], BF16, stack=stack, base="l2c_t1")
        tile2 = ctx.sbuf([128, 2, 20 * 128], BF16, stack=stack, base="l2c_t2")
        tstage = ctx.sbuf([128, 2, 512], F32, stack=stack, base="l2c_ts")
        t1ld = SlotSems(ctx, stack, 2, kind="sw")
        t2ld = SlotSems(ctx, stack, 2, kind="sw")
        tst = SlotSems(ctx, stack, 2, kind="hw")
        with nc.Block() as b:
            @b.vector
            def _(v):
                v.memset(up[:, :, 0:4], 0.0)
                v.memset(up[:, :, 2052:2056], 0.0)
                for c in range(NCH):
                    v.tensor_copy(out=up[:, c, 4:2052], in_=XLN[:, c, :])
        for tb in range(4):
            t0 = tb * 512
            with contextlib.ExitStack() as bs:
                sem_mm = ctx.sem(bs)
                sem_ev = ctx.sem(bs)
                ps = ctx.psum([128, 2, 1024], F32, stack=bs, base="l2c_ps")
                with nc.Block() as b:
                    @b.gpsimd
                    def _(g_):
                        for och in range(32):
                            idx = tb * 32 + och
                            if och >= 2:
                                g_.wait_ge(sem_mm, och - 1)
                            t1ld.inc(g_.dma_start(
                                out=tile1[:, och % 2, :],
                                in_=ctx.dram["l2a_c1t"][och, :, :].rearrange(
                                    "(t k) m -> k t m", k=128)), och % 2)
                    @b.tensor
                    def _(t):
                        for och in range(32):
                            idx = tb * 32 + och
                            t1ld.wait(t, idx % 2, 16 * (idx // 2 + 1))
                            if och >= 2:
                                t.wait_ge(sem_ev, och - 1)
                            for (lo, hi) in NSPL:
                                for tap in range(5):
                                    t.matmul(ps[:, och % 2, lo:hi],
                                             tile1[:, och % 2, tap * 128:(tap + 1) * 128],
                                             up[:, och // 4, t0 + tap + lo:t0 + tap + hi],
                                             start=(tap == 0), stop=False,
                                             skip_group_check=True)
                                mm = t.matmul(ps[:, och % 2, lo:hi],
                                              c1b[0:1, och * 128:(och + 1) * 128],
                                              ctx.consts["ones_row512"][0:1, 0:hi - lo],
                                              start=False, stop=True, skip_group_check=True)
                            mm.then_inc(sem_mm, 1)
                    @b.scalar
                    def _(s):
                        for och in range(32):
                            s.wait_ge(sem_mm, och + 1)
                            s.activation(out=h1[:, och, 0:516], in_=ps[:, och % 2, 0:516],
                                         func=AF.Gelu).then_inc(sem_ev, 1)
                if tb == 0 or tb == 3:
                    with nc.Block() as b:
                        @b.vector
                        def _(v):
                            if tb == 0:
                                v.memset(h1[:, :, 0:2], 0.0)
                            else:
                                v.memset(h1[:, :, 514:516], 0.0)
            with contextlib.ExitStack() as bs:
                sem_mm = ctx.sem(bs)
                sem_ev = ctx.sem(bs)
                ps = ctx.psum([128, 2, 512], F32, stack=bs, base="l2c_ps2")
                with nc.Block() as b:
                    @b.gpsimd
                    def _(g_):
                        for oc2 in range(8):
                            if oc2 >= 2:
                                g_.wait_ge(sem_mm, oc2 - 1)
                            t2ld.inc(g_.dma_start(
                                out=tile2[:, oc2 % 2, :],
                                in_=ctx.dram["l2a_c2t"][oc2, :, :].rearrange(
                                    "(t k) m -> k t m", k=128)), oc2 % 2)
                    @b.tensor
                    def _(t):
                        for oc2 in range(8):
                            idx = tb * 8 + oc2
                            t2ld.wait(t, idx % 2, 16 * (idx // 2 + 1))
                            if oc2 >= 2:
                                t.wait_ge(sem_ev, oc2 - 1)
                            for ks in range(4):
                                for tap in range(5):
                                    t.matmul(ps[:, oc2 % 2, :],
                                             tile2[:, oc2 % 2, (ks * 5 + tap) * 128:(ks * 5 + tap + 1) * 128],
                                             h1[:, 4 * oc2 + ks, tap:tap + 512],
                                             start=(ks == 0 and tap == 0), stop=False,
                                             skip_group_check=True)
                            mm2 = t.matmul(ps[:, oc2 % 2, :],
                                           c2b[0:1, oc2 * 128:(oc2 + 1) * 128],
                                           ctx.consts["ones_row512"][0:1, :],
                                           start=False, stop=True, skip_group_check=True)
                            mm2.then_inc(sem_mm, 1)
                    @b.vector
                    def _(v):
                        for oc2 in range(8):
                            idx = tb * 8 + oc2
                            v.wait_ge(sem_mm, oc2 + 1)
                            if idx >= 2:
                                tst.wait(v, idx % 2, 16 * ((idx - 2) // 2 + 1))
                            v.tensor_add(out=tstage[:, oc2 % 2, :], in0=ps[:, oc2 % 2, :],
                                         in1=XLN[:, oc2, t0:t0 + 512]).then_inc(sem_ev, 1)
                    @b.sync
                    def _(sy):
                        for oc2 in range(8):
                            idx = tb * 8 + oc2
                            sy.wait_ge(sem_ev, oc2 + 1)
                            tst.inc(sy.dma_start(out=t_dram[:, oc2, t0:t0 + 512],
                                                 in_=tstage[:, oc2 % 2, :]), idx % 2)
        tst.barrier(0, 16 * 16)
        tst.barrier(1, 16 * 16)


def _l2a_stream_affine_ln(ctx, t_dram, h_dram, hbf, gname, bname, it, Xacc,
                          eps=1e-12):
    nc = ctx.nc
    with contextlib.ExitStack() as stack:
        gc = ctx.consts[gname] if it is None else None
        bc = ctx.consts[bname] if it is None else None
        if it is not None:
            gc = ctx.sbuf([128, 8], F32, stack=stack, base="lnA_g")
            bc = ctx.sbuf([128, 8], F32, stack=stack, base="lnA_b")
            dl = ctx.dsem(stack, "sw")
            with nc.Block() as b:
                @b.gpsimd
                def _(g_):
                    dl.inc(g_.dma_start(out=gc[:], in_=ctx.dram[gname][it]))
                    dl.inc(g_.dma_start(out=bc[:], in_=ctx.dram[bname][it]))
            dl.barrier(32)
        tch = ctx.sbuf([128, 2, 512], F32, stack=stack, base="lns_t")
        tbf = ctx.sbuf([128, 2, 512], BF16, stack=stack, base="lns_tb")
        x2b = ctx.sbuf([128, 2, 512], BF16, stack=stack, base="lns_x2")
        mu = ctx.sbuf([1, T], F32, stack=stack, base="lns_mu")
        ms = ctx.sbuf([1, T], F32, stack=stack, base="lns_ms")
        stage = ctx.sbuf([128, 2, 512], F32, stack=stack, base="lns_st")
        tld = SlotSems(ctx, stack, 2, kind="hw")
        sst = SlotSems(ctx, stack, 2, kind="sw")
        es = ESync(ctx, stack)
        with contextlib.ExitStack() as ps_:
            ps_sx = ctx.psum([1, T], F32, stack=ps_, base="lns_psx")
            ps_sx2 = ctx.psum([1, T], F32, stack=ps_, base="lns_psx2")
            sem_cv = ctx.sem(ps_)
            sem_sq = ctx.sem(ps_)
            sem_mm = ctx.sem(ps_)
            with nc.Block() as b:
                @b.sync
                def _(sy):
                    for nn in range(32):
                        tb, c = nn // 8, nn % 8
                        if nn >= 2:
                            sy.wait_ge(sem_cv, nn - 1)
                            sy.wait_ge(sem_sq, nn - 1)
                        tld.inc(sy.dma_start(out=tch[:, nn % 2, :],
                                             in_=t_dram[:, c, tb * 512:(tb + 1) * 512]),
                                nn % 2)
                @b.vector
                def _(v):
                    for nn in range(32):
                        tld.wait(v, nn % 2, 16 * (nn // 2 + 1))
                        if nn >= 2:
                            v.wait_ge(sem_mm, nn - 1)
                        v.tensor_copy(out=tbf[:, nn % 2, :],
                                      in_=tch[:, nn % 2, :]).then_inc(sem_cv, 1)
                @b.scalar
                def _(s):
                    for nn in range(32):
                        tld.wait(s, nn % 2, 16 * (nn // 2 + 1))
                        if nn >= 2:
                            s.wait_ge(sem_mm, nn - 1)
                        s.activation(out=x2b[:, nn % 2, :], in_=tch[:, nn % 2, :],
                                     func=AF.Square).then_inc(sem_sq, 1)
                @b.tensor
                def _(t):
                    oc = ctx.consts["ones_col_bf"]
                    for nn in range(32):
                        tb, c = nn // 8, nn % 8
                        f0 = tb * 512
                        t.wait_ge(sem_cv, nn + 1)
                        t.wait_ge(sem_sq, nn + 1)
                        t.matmul(ps_sx[0:1, f0:f0 + 512], oc[:, 0:1], tbf[:, nn % 2, :],
                                 start=(c == 0), stop=(c == 7), skip_group_check=True)
                        mm = t.matmul(ps_sx2[0:1, f0:f0 + 512], oc[:, 0:1], x2b[:, nn % 2, :],
                                      start=(c == 0), stop=(c == 7), skip_group_check=True)
                        mm.then_inc(sem_mm, 1)
            with nc.Block() as b:
                @b.vector
                def _(v):
                    v.tensor_scalar_mul(out=mu[:], in0=ps_sx[0:1, :], scalar1=1.0 / D)
                    v.tensor_scalar_mul(out=ms[:], in0=ps_sx2[0:1, :], scalar1=1.0 / D)
        _var_rstd_rows(ctx, mu, ms, eps)
        with contextlib.ExitStack() as ps_:
            ps_b = ctx.psum([128, 2, 512], F32, stack=ps_, base="lns_psb")
            sem_bb = ctx.sem(ps_)
            sem_ap = ctx.sem(ps_)
            with nc.Block() as b:
                @b.sync
                def _(sy):
                    for nn in range(32):
                        tb, c = nn // 8, nn % 8
                        if nn >= 2:
                            sy.wait_ge(sem_ap, nn - 1)
                        tld.inc(sy.dma_start(out=tch[:, nn % 2, :],
                                             in_=t_dram[:, c, tb * 512:(tb + 1) * 512]),
                                nn % 2)
                @b.tensor
                def _(t):
                    orow = ctx.consts["ones_row_f32"]
                    for tb in range(4):
                        f0 = tb * 512
                        t.wait_ge(sem_ap, tb * 8)
                        t.matmul(ps_b[:, 0, :], orow[0:1, :], mu[0:1, f0:f0 + 512],
                                 start=True, stop=True, skip_group_check=True)
                        t.matmul(ps_b[:, 1, :], orow[0:1, :], ms[0:1, f0:f0 + 512],
                                 start=True, stop=True,
                                 skip_group_check=True).then_inc(sem_bb, 1)
                @b.vector
                def _(v):
                    for nn in range(32):
                        tb, c = nn // 8, nn % 8
                        tld.wait(v, nn % 2, 16 * (16 + nn // 2 + 1))
                        v.wait_ge(sem_bb, tb + 1)
                        i = v.tensor_sub(out=tch[:, nn % 2, :], in0=tch[:, nn % 2, :],
                                         in1=ps_b[:, 0, :])
                        es.edge(v, i)
                        i = v.tensor_mul(out=tch[:, nn % 2, :], in0=tch[:, nn % 2, :],
                                         in1=ps_b[:, 1, :])
                        es.edge(v, i)
                        if Xacc is None:
                            if nn >= 2:
                                sst.wait(v, nn % 2, 16 * ((nn - 2) // 2 + 1))
                            i = v.tensor_scalar(out=stage[:, nn % 2, :], in0=tch[:, nn % 2, :],
                                                scalar1=gc[:, c:c + 1], scalar2=bc[:, c:c + 1],
                                                op0=ALU.mult, op1=ALU.add)
                            es.edge(v, i)
                            i = v.tensor_copy(out=hbf[:, c, tb * 512:(tb + 1) * 512],
                                              in_=stage[:, nn % 2, :])
                            i.then_inc(sem_ap, 1)
                        else:
                            i = v.scalar_tensor_tensor(out=Xacc[:, c, tb * 512:(tb + 1) * 512],
                                                       in0=tch[:, nn % 2, :],
                                                       scalar=gc[:, c:c + 1],
                                                       in1=Xacc[:, c, tb * 512:(tb + 1) * 512],
                                                       op0=ALU.mult, op1=ALU.add)
                            es.edge(v, i)
                            i = v.tensor_scalar(out=Xacc[:, c, tb * 512:(tb + 1) * 512],
                                                in0=Xacc[:, c, tb * 512:(tb + 1) * 512],
                                                scalar1=bc[:, c:c + 1], scalar2=None,
                                                op0=ALU.add)
                            es.edge(v, i)
                            v.sem_inc(sem_ap, 1)
                @b.gpsimd
                def _(g_):
                    if Xacc is None:
                        for nn in range(32):
                            tb, c = nn // 8, nn % 8
                            g_.wait_ge(sem_ap, nn + 1)
                            sst.inc(g_.dma_start(out=h_dram[:, c, tb * 512:(tb + 1) * 512],
                                                 in_=stage[:, nn % 2, :]), nn % 2)
        if Xacc is None:
            sst.barrier(0, 16 * 16)
            sst.barrier(1, 16 * 16)


def _var_rstd_rows(ctx, mu, ms, eps):
    nc = ctx.nc
    with contextlib.ExitStack() as stack:
        sq = ctx.sbuf([1, T], F32, stack=stack, base="vr_sq")
        es = ESync(ctx, stack)
        with nc.Block() as b:
            @b.vector
            def _(v):
                i = v.tensor_mul(out=sq[:], in0=mu[:], in1=mu[:])
                es.edge(v, i)
                v.tensor_sub(out=ms[:], in0=ms[:], in1=sq[:])
        with nc.Block() as b:
            @b.scalar
            def _(s):
                i = s.activation(out=ms[:], in_=ms[:], func=AF.Ln, bias=float(eps))
                es.edge(s, i)
                s.activation(out=ms[:], in_=ms[:], func=AF.Exp, scale=-0.5)


def _l2a_iter(ctx, hbf, h_dram, t_dram, it):
    nc = ctx.nc
    with contextlib.ExitStack() as stack:
        kv_sb = ctx.sbuf([128, 8, 64], BF16, stack=stack, base="l2i_kv")
        kmean = ctx.sbuf([128, 8], F32, stack=stack, base="l2i_km")
        esv = ESync(ctx, stack)
        kvscope = stack.enter_context(contextlib.ExitStack())
        ktok = ctx.sbuf([128, 16, 1024], BF16, stack=kvscope, base="l2i_k")
        vtok = ctx.sbuf([128, 16, 1024], BF16, stack=kvscope, base="l2i_v")
        for (wn, bn, buf, gelu) in (("l2a_wk", "l2a_bk", ktok, True),
                                    ("l2a_wv", "l2a_bv", vtok, False)):
            with contextlib.ExitStack() as ks:
                wc = ctx.sbuf([128, NCH, 1024], BF16, stack=ks, base="l2i_w")
                brow = ctx.sbuf([1, 1024], BF16, stack=ks, base="l2i_b")
                dl = ctx.dsem(ks, "sw")
                with nc.Block() as b:
                    @b.gpsimd
                    def _(g_):
                        dl.inc(g_.dma_start(out=wc[:], in_=ctx.dram[wn][it]))
                        dl.inc(g_.dma_start(out=brow[:], in_=ctx.dram[bn][it]))
                dl.barrier(32)
                sem_mm = ctx.sem(ks)
                sem_ev = ctx.sem(ks)
                ps = ctx.psum([128, 2, 512], F32, stack=ks, base="l2i_ps")
                with nc.Block() as b:
                    @b.tensor
                    def _(t):
                        for nn in range(32):
                            tc, eb = nn // 2, nn % 2
                            if nn >= 2:
                                t.wait_ge(sem_ev, nn - 1)
                            for dc in range(NCH):
                                t.matmul(ps[:, nn % 2, :], hbf[:, dc, tc * 128:(tc + 1) * 128],
                                         wc[:, dc, eb * 512:(eb + 1) * 512],
                                         start=(dc == 0), stop=False, skip_group_check=True)
                            mm = t.matmul(ps[:, nn % 2, :], ctx.consts["ones_row_bf"][0:1, :],
                                          brow[0:1, eb * 512:(eb + 1) * 512],
                                          start=False, stop=True, skip_group_check=True)
                            mm.then_inc(sem_mm, 1)
                    @b.scalar
                    def _(s):
                        for nn in range(32):
                            tc, eb = nn // 2, nn % 2
                            s.wait_ge(sem_mm, nn + 1)
                            s.activation(out=buf[:, tc, eb * 512:(eb + 1) * 512],
                                         in_=ps[:, nn % 2, :],
                                         func=(AF.Gelu if gelu else AF.Copy)).then_inc(sem_ev, 1)
            if gelu:
                with nc.Block() as b:
                    @b.vector
                    def _(v):
                        for tc in range(16):
                            v.tensor_scalar(out=buf[:, tc, :], in0=buf[:, tc, :],
                                            scalar1=0.21, scalar2=None, op0=ALU.add)
        with contextlib.ExitStack() as ks:
            ps_km = ctx.psum([128, 8], F32, stack=ks, base="l2i_pskm")
            with nc.Block() as b:
                @b.tensor
                def _(t):
                    for ds in range(8):
                        for tc in range(16):
                            t.matmul(ps_km[:, ds:ds + 1],
                                     ktok[:, tc, ds * 128:(ds + 1) * 128],
                                     ctx.consts["ones_col_bf"][:, 0:1],
                                     start=(tc == 0), stop=(tc == 15),
                                     skip_group_check=True)
            with nc.Block() as b:
                @b.vector
                def _(v):
                    v.tensor_scalar_mul(out=kmean[:], in0=ps_km[:, :], scalar1=1.0 / T)
        with contextlib.ExitStack() as ks:
            ctb = ctx.sbuf([128, 2, 2, 512], BF16, stack=ks, base="l2i_ctb")
            mbuf = ctx.sbuf([128, 4, 512], F32, stack=ks, base="l2i_m")
            cl = SlotSems(ctx, ks, 2, kind="sw")
            es = ESync(ctx, ks)
            sem_ro = ctx.sem(ks)
            with nc.Block() as b:
                @b.gpsimd
                def _(g_):
                    for tc in range(16):
                        if tc >= 2:
                            g_.wait_ge(sem_ro, tc - 1)
                        cl.inc(g_.dma_start(out=ctb[:, tc % 2, 0, :],
                                            in_=ctx.dram["l2a_Ct"][:, tc * 512:(tc + 1) * 512]), tc % 2)
                        cl.inc(g_.dma_start(out=ctb[:, tc % 2, 1, :],
                                            in_=ctx.dram["l2a_St"][:, tc * 512:(tc + 1) * 512]), tc % 2)
                @b.vector
                def _(v):
                    for tc in range(16):
                        cl.wait(v, tc % 2, 32 * (tc // 2 + 1))
                        e = ktok[:, tc, 0:1024:2]
                        o = ktok[:, tc, 1:1024:2]
                        v.tensor_mul(out=mbuf[:, 0, :], in0=e, in1=ctb[:, tc % 2, 0, :])
                        v.tensor_mul(out=mbuf[:, 1, :], in0=o, in1=ctb[:, tc % 2, 1, :])
                        v.tensor_mul(out=mbuf[:, 2, :], in0=e, in1=ctb[:, tc % 2, 1, :])
                        i = v.tensor_mul(out=mbuf[:, 3, :], in0=o, in1=ctb[:, tc % 2, 0, :])
                        es.edge(v, i)
                        v.tensor_sub(out=e, in0=mbuf[:, 0, :], in1=mbuf[:, 1, :])
                        i = v.tensor_add(out=o, in0=mbuf[:, 2, :], in1=mbuf[:, 3, :])
                        es.edge(v, i)
                        v.sem_inc(sem_ro, 1)
        with contextlib.ExitStack() as ks:
            ps_kv = ctx.psum([128, 8, 64], F32, stack=ks, base="l2i_pskv")
            with nc.Block() as b:
                @b.tensor
                def _(t):
                    for h in range(16):
                        rb = (h % 2) * 64
                        for tc in range(16):
                            t.matmul(ps_kv[rb:rb + 64, h // 2, :],
                                     ktok[:, tc, h * 64:(h + 1) * 64],
                                     vtok[:, tc, h * 64:(h + 1) * 64],
                                     start=(tc == 0), stop=(tc == 15),
                                     skip_group_check=True)
            with nc.Block() as b:
                @b.scalar
                def _(s):
                    s.activation(out=kv_sb[:, :, :], in_=ps_kv[:, :, :], func=AF.Copy,
                                 scale=1.0 / T)
        kvscope.close()
        with contextlib.ExitStack() as qs:
            wq = ctx.sbuf([128, NCH, 1024], BF16, stack=qs, base="l2i_wq")
            bqrow = ctx.sbuf([1, 1024], BF16, stack=qs, base="l2i_bq")
            dl = ctx.dsem(qs, "sw")
            with nc.Block() as b:
                @b.gpsimd
                def _(g_):
                    dl.inc(g_.dma_start(out=wq[:], in_=ctx.dram["l2a_wq"][it]))
                    dl.inc(g_.dma_start(out=bqrow[:], in_=ctx.dram["l2a_bq"][it]))
            dl.barrier(32)
            qbuf = ctx.sbuf([128, 2048], BF16, stack=qs, base="l2i_qb")
            qrbuf = ctx.sbuf([128, 2048], BF16, stack=qs, base="l2i_qr")
            kmd = ctx.sbuf([128, 2], BF16, stack=qs, base="l2i_kmd")
            kmdf = ctx.sbuf([128, 2], F32, stack=qs, base="l2i_kmdf")
            zb_sb = ctx.sbuf([128, 2048], F32, stack=qs, base="l2i_zb")
            attf = ctx.sbuf([128, 2048], F32, stack=qs, base="l2i_att")
            zrow = attf[0:2, :]
            hch = ctx.sbuf([128, 2048], F32, stack=qs, base="l2i_hch")
            c12 = ctx.sbuf([128, 2, 2048], BF16, stack=qs, base="l2i_c12")
            hld = SlotSems(ctx, qs, 2, kind="hw")
            c12ld = SlotSems(ctx, qs, 2, kind="sw")
            tstq = SlotSems(ctx, qs, 2, kind="hw")
            hmask = ctx.consts["l2a_hmask"]
            ind2 = ctx.consts["l2a_ind2"]
            psw = ctx.consts["l2a_pswap"]
            sem_rope = ctx.sem(qs)
            for c in range(8):
                with contextlib.ExitStack() as cs_:
                    es = ESync(ctx, cs_)
                    ps_a = ctx.psum([128, 2048], F32, stack=cs_, base="l2i_psa")
                    with nc.Block() as b:
                        @b.gpsimd
                        def _(g_):
                            if c >= 1:
                                g_.wait_ge(sem_rope, c)
                            c12ld.inc(g_.dma_start(out=c12[:, 0, :],
                                                   in_=ctx.dram["l2a_C1"][:, c * T:(c + 1) * T]), 0)
                            c12ld.inc(g_.dma_start(out=c12[:, 1, :],
                                                   in_=ctx.dram["l2a_C2"][:, c * T:(c + 1) * T]), 0)
                        @b.sync
                        def _(sy):
                            hld.inc(sy.dma_start(out=hch[:], in_=h_dram[:, c, :]), c % 2)
                    with nc.Block() as b:
                        @b.tensor
                        def _(t):
                            for nb in range(4):
                                for dc in range(NCH):
                                    t.matmul(ps_a[:, nb * 512:(nb + 1) * 512],
                                             wq[:, dc, c * 128:(c + 1) * 128],
                                             hbf[:, dc, nb * 512:(nb + 1) * 512],
                                             start=(dc == 0), stop=False, skip_group_check=True)
                                t.matmul(ps_a[:, nb * 512:(nb + 1) * 512],
                                         bqrow[0:1, c * 128:(c + 1) * 128],
                                         ctx.consts["ones_row512"][0:1, :],
                                         start=False, stop=True, skip_group_check=True)
                    with nc.Block() as b:
                        @b.scalar
                        def _(s):
                            s.activation(out=qbuf[:], in_=ps_a[:, :], func=AF.Gelu)
                    with nc.Block() as b:
                        @b.vector
                        def _(v):
                            v.tensor_scalar(out=qbuf[:], in0=qbuf[:], scalar1=0.21,
                                            scalar2=None, op0=ALU.add)
                            i = v.tensor_mul(out=kmdf[:, 0:1], in0=kmean[:, c:c + 1],
                                             in1=hmask[:, 0:1])
                            i = v.tensor_mul(out=kmdf[:, 1:2], in0=kmean[:, c:c + 1],
                                             in1=hmask[:, 1:2])
                            es.edge(v, i)
                            v.tensor_copy(out=kmd[:], in_=kmdf[:])
                    with nc.Block() as b:
                        @b.tensor
                        def _(t):
                            for nb in range(4):
                                t.matmul(ps_a[0:2, nb * 512:(nb + 1) * 512], kmd[:, :],
                                         qbuf[:, nb * 512:(nb + 1) * 512],
                                         start=True, stop=True, skip_group_check=True)
                    with nc.Block() as b:
                        @b.vector
                        def _(v):
                            if c >= 1:
                                tstq.wait(v, (c - 1) % 2, 16 * ((c - 1) // 2 + 1))
                            i = v.tensor_scalar(out=ps_a[0:2, :], in0=ps_a[0:2, :],
                                                scalar1=1e-6, scalar2=None, op0=ALU.add)
                            es.edge(v, i)
                            v.reciprocal(out=zrow[:, :], in_=ps_a[0:2, :])
                    with nc.Block() as b:
                        @b.tensor
                        def _(t):
                            for nb in range(4):
                                t.matmul(ps_a[:, nb * 512:(nb + 1) * 512], ind2[:, :],
                                         zrow[:, nb * 512:(nb + 1) * 512],
                                         start=True, stop=True, skip_group_check=True)
                    with nc.Block() as b:
                        @b.scalar
                        def _(s):
                            s.activation(out=zb_sb[:], in_=ps_a[:, :], func=AF.Copy)
                    with nc.Block() as b:
                        @b.tensor
                        def _(t):
                            for nb in range(4):
                                t.matmul(ps_a[:, nb * 512:(nb + 1) * 512], psw[:, :],
                                         qbuf[:, nb * 512:(nb + 1) * 512],
                                         start=True, stop=True, skip_group_check=True)
                    with nc.Block() as b:
                        @b.vector
                        def _(v):
                            c12ld.wait(v, 0, 32 * (c + 1))
                            i = v.tensor_mul(out=attf[:, :], in0=qbuf[:, :], in1=c12[:, 0, :])
                            i = v.tensor_mul(out=qrbuf[:, :], in0=ps_a[:, :], in1=c12[:, 1, :])
                            es.edge(v, i)
                            i = v.tensor_add(out=qrbuf[:, :], in0=attf[:, :], in1=qrbuf[:, :])
                            es.edge(v, i)
                            v.sem_inc(sem_rope, 1)
                    with nc.Block() as b:
                        @b.tensor
                        def _(t):
                            for hh in range(2):
                                rb = hh * 64
                                for nb in range(4):
                                    t.matmul(ps_a[rb:rb + 64, nb * 512:(nb + 1) * 512],
                                             kv_sb[rb:rb + 64, c, :],
                                             qrbuf[rb:rb + 64, nb * 512:(nb + 1) * 512],
                                             start=True, stop=True, skip_group_check=True)
                    with nc.Block() as b:
                        @b.vector
                        def _(v):
                            hld.wait(v, c % 2, 16 * (c // 2 + 1))
                            i = v.tensor_mul(out=attf[:, :], in0=ps_a[:, :], in1=zb_sb[:, :])
                            es.edge(v, i)
                            i = v.tensor_add(out=attf[:, :], in0=attf[:, :], in1=hch[:, :])
                            es.edge(v, i)
                    with nc.Block() as b:
                        @b.sync
                        def _(sy):
                            tstq.inc(sy.dma_start(out=t_dram[:, c, :], in_=attf[:, :]), c % 2)
            tstq.barrier(0, 16 * 4)
            tstq.barrier(1, 16 * 4)


def _l2a_ffn(ctx, X, hbf, h_dram, t_dram):
    nc = ctx.nc
    NF = 32
    TBN = 256
    NTB = T // TBN
    with contextlib.ExitStack() as stack:
        w1b = ctx.sbuf([128, 2, NCH, 128], BF16, stack=stack, base="l2f_w1b")
        w2b = ctx.sbuf([128, 2, 1024], BF16, stack=stack, base="l2f_w2b")
        hbuf2 = ctx.sbuf([128, 2, TBN], BF16, stack=stack, base="l2f_hb")
        hldb = ctx.sbuf([128, 2, NCH, TBN], F32, stack=stack, base="l2f_hl")
        tstage = ctx.sbuf([128, 2, NCH, TBN], F32, stack=stack, base="l2f_tst")
        b1c = _load_const_scoped(ctx, stack, "l2a_b1c", F32)
        b2c = _load_const_scoped(ctx, stack, "l2a_b2c", F32)
        ps_h = ctx.psum([128, 2, 512], F32, stack=stack, base="l2f_psh")
        ps_y = ctx.psum([128, 4, 512], F32, stack=stack, base="l2f_psy")
        wld = SlotSems(ctx, stack, 2, kind="sw")
        hld = SlotSems(ctx, stack, 2, kind="hw")
        tst = SlotSems(ctx, stack, 2, kind="hw")
        sem_hmm = ctx.sem(stack)
        sem_ymm = ctx.sem(stack)
        sem_ev = ctx.sem(stack)
        sem_xe = ctx.sem(stack)
        esv = ESync(ctx, stack)
        w1src = ctx.dram["l2a_w1T"]
        w2src = ctx.dram["l2a_w2T"]
        for tb in range(NTB):
            t0 = tb * TBN
            with nc.Block() as b:
                @b.sync
                def _(sy):
                    if tb >= 2:
                        sy.wait_ge(sem_xe, tb - 1)
                    hld.inc(sy.dma_start(out=hldb[:, tb % 2, :, :],
                                         in_=h_dram[:, :, t0:t0 + TBN]), tb % 2)
                @b.gpsimd
                def _(g):
                    for f in range(NF):
                        idx = tb * NF + f
                        if f >= 2:
                            g.wait_ge(sem_ymm, idx - 1)
                        wld.inc(g.dma_start(out=w1b[:, f % 2, :, :],
                                            in_=w1src[:, :, f * 128:(f + 1) * 128]), f % 2)
                        wld.inc(g.dma_start(out=w2b[:, f % 2, :],
                                            in_=w2src[:, f, :]), f % 2)
                @b.tensor
                def _(t):
                    for f in range(NF):
                        idx = tb * NF + f
                        wld.wait(t, f % 2, 32 * (idx // 2 + 1))
                        if f >= 2:
                            t.wait_ge(sem_ev, idx - 1)
                        for dc in range(NCH):
                            mm = t.matmul(ps_h[:, f % 2, 0:TBN], w1b[:, f % 2, dc, :],
                                          hbf[:, dc, t0:t0 + TBN],
                                          start=(dc == 0), stop=(dc == NCH - 1),
                                          skip_group_check=True)
                        mm.then_inc(sem_hmm, 1)
                        if f == 1:
                            for j in range(4):
                                t.matmul(ps_y[:, j, :], ctx.consts["ones_row_bf"][0:1, :],
                                         ctx.consts["zeros_row512"][0:1, :],
                                         start=True, stop=False, skip_group_check=True)
                        if f >= 1:
                            t.wait_ge(sem_ev, idx)
                            for hc in range(NCH):
                                mm = t.matmul(ps_y[:, hc // 2, (hc % 2) * TBN:(hc % 2 + 1) * TBN],
                                              w2b[:, (f - 1) % 2, hc * 128:(hc + 1) * 128],
                                              hbuf2[:, (f - 1) % 2, :],
                                              start=False, stop=False,
                                              skip_group_check=True)
                            mm.then_inc(sem_ymm, 1)
                    t.wait_ge(sem_ev, tb * NF + NF)
                    for hc in range(NCH):
                        mm = t.matmul(ps_y[:, hc // 2, (hc % 2) * TBN:(hc % 2 + 1) * TBN],
                                      w2b[:, (NF - 1) % 2, hc * 128:(hc + 1) * 128],
                                      hbuf2[:, (NF - 1) % 2, :],
                                      start=False, stop=True, skip_group_check=True)
                    mm.then_inc(sem_ymm, 1)
                @b.scalar
                def _(s):
                    for f in range(NF):
                        idx = tb * NF + f
                        s.wait_ge(sem_hmm, idx + 1)
                        if f >= 2:
                            s.wait_ge(sem_ymm, idx - 1)
                        s.activation(out=hbuf2[:, f % 2, :], in_=ps_h[:, f % 2, 0:TBN],
                                     func=AF.Gelu, bias=b1c[:, f:f + 1]).then_inc(sem_ev, 1)
            with nc.Block() as b:
                @b.vector
                def _(v):
                    hld.wait(v, tb % 2, 16 * (tb // 2 + 1))
                    if tb >= 2:
                        tst.wait(v, tb % 2, 16 * ((tb - 2) // 2 + 1))
                    for hc in range(NCH):
                        i = v.scalar_tensor_tensor(
                            out=tstage[:, tb % 2, hc, :],
                            in0=ps_y[:, hc // 2, (hc % 2) * TBN:(hc % 2 + 1) * TBN],
                            scalar=b2c[:, hc:hc + 1],
                            in1=hldb[:, tb % 2, hc, :], op0=ALU.add, op1=ALU.add)
                    esv.edge(v, i)
                    v.sem_inc(sem_xe, 1)
                @b.sync
                def _(sy):
                    sy.wait_ge(sem_xe, tb + 1)
                    tst.inc(sy.dma_start(out=t_dram[:, :, t0:t0 + TBN],
                                         in_=tstage[:, tb % 2, :, :]), tb % 2)
        tst.barrier(0, 16 * 4)
        tst.barrier(1, 16 * 4)


from concourse.bass_utils import run_bass_kernel_spmd


def _body_l0f(ctx, X, XLN):
    load_consts(ctx, L0_FFN_CONSTS, L0_FFN_CONST_DTYPES)
    emit_glue_ln(ctx, X, XLN)
    emit_l0_ffn(ctx, X, XLN)


def _body_l0a(ctx, X, XLN):
    load_consts(ctx, L0_ATTN_CONSTS, L0_ATTN_CONST_DTYPES)
    emit_glue_ln(ctx, X, XLN)
    emit_l0_attn(ctx, X, XLN)


def _body_l1f(ctx, X, XLN):
    load_consts(ctx, ["l1_wTr", "l1_dishab", "l1_poolr"], L1_FFN_CONST_DTYPES)
    emit_glue_ln(ctx, X, XLN)
    emit_l1_ffn(ctx, X, XLN)


def _body_l1a(ctx, X, XLN):
    load_consts(ctx, L1_ATTN_CONSTS, L1_ATTN_CONST_DTYPES)
    emit_glue_ln(ctx, X, XLN)
    emit_l1_attn(ctx, X, XLN)


def _body_l2f(ctx, X, XLN):
    nc = ctx.nc
    load_consts(ctx, ["ltmask", "ident128"] + L2_FFN_CONSTS, L2_FFN_CONST_DTYPES)
    scr1 = nc.dram_tensor("l2scr_w1", [8, 1024, 2048], BF16, kind="Internal").ap()
    scr2 = nc.dram_tensor("l2scr_w2", [8, 2048, 1024], BF16, kind="Internal").ap()
    wbar_bf = ctx.sbuf([128, 8, 8], BF16, base="wbarbf")
    bbar_bf = ctx.sbuf([1, 8], BF16, base="bbarbf")
    emit_l2_ffn_prep(ctx, scr1, scr2, wbar_bf, bbar_bf)
    emit_glue_ln(ctx, X, XLN)
    emit_l2_ffn(ctx, X, XLN, scr1, scr2, wbar_bf, bbar_bf)


def _body_l2a(ctx, X, XLN):
    load_consts(ctx, L2_ATTN_CONSTS, L2_ATTN_CONST_DTYPES)
    emit_glue_ln(ctx, X, XLN)
    emit_l2_attn(ctx, X, XLN)


def _body_l3f(ctx, X, XLN):
    nc = ctx.nc
    scr1 = nc.dram_tensor("scr_w1t", [1024, 4096], BF16, kind="Internal").ap()
    scr2 = nc.dram_tensor("scr_w2t", [4096, 1024], BF16, kind="Internal").ap()
    emit_l3_ffn_prep(ctx, scr1, scr2)
    emit_glue_ln(ctx, X, XLN)
    emit_l3_ffn(ctx, X, XLN, scr1, scr2)


def _body_l3a(ctx, X, XLN):
    load_consts(ctx, ["ident128"] + L3_ATTN_CONSTS, L3_ATTN_CONST_DTYPES)
    emit_glue_ln(ctx, X, XLN)
    emit_l3_attn(ctx, X, XLN)


_BODIES = {
    "l0f": _body_l0f, "l0a": _body_l0a, "l1f": _body_l1f, "l1a": _body_l1a,
    "l2f": _body_l2f, "l2a": _body_l2a, "l3f": _body_l3f, "l3a": _body_l3a,
}

_SEQ = ["l0f", "l0a", "l0f",
        "l1f", "l1a", "l1f",
        "l2f", "l2a", "l2f",
        "l3f", "l3a", "l3f"]


def _fam_arrays(inputs):
    def f32(k):
        return np.ascontiguousarray(np.asarray(inputs[k], np.float32))

    base = base_consts()
    base["x_fm"] = np.zeros((128, NCH, T), np.float32)
    _keep = {
        "l0f": ["ones_col_bf", "ones_row_f32", "ones_row_bf"],
        "l0a": ["ones_col_bf", "ones_row_f32", "ones_row_bf", "zeros_row512",
                "ones_row512", "ident128_bf"],
        "l1f": ["ones_col_bf", "ones_row_f32"],
        "l1a": ["ones_col_bf", "ones_row_f32", "ones_row_bf", "ones_row512"],
        "l2f": ["ones_col_bf", "ones_row_f32", "ones_col_f32", "ones_row_bf",
                "ones_row512", "negones_row_f32"],
        "l2a": ["ones_col_bf", "ones_row_f32", "ones_row_bf", "ones_row512",
                "zeros_row512"],
        "l3f": ["ones_col_bf", "ones_row_f32", "ones_col_f32", "ones_row_bf",
                "zeros_row512"],
        "l3a": ["ones_col_bf", "ones_row_f32", "ones_row_bf",
                "negones_row_f32", "ones_sq"],
    }

    def _base_for(fam):
        d = {k: base[k] for k in _keep[fam]}
        d["x_fm"] = base["x_fm"]
        return d
    fams = {}

    a = _base_for("l0f")
    a.update(l0_ffn_host_arrays(f32("L0_ffn_Wr"), f32("L0_ffn_br"),
                                f32("L0_ffn_W1"), f32("L0_ffn_b1"),
                                f32("L0_ffn_W2"), f32("L0_ffn_b2")))
    fams["l0f"] = a

    a = _base_for("l0a")
    a.update(l0_attn_host_arrays(f32("L0_attn_Wq"), f32("L0_attn_bq"),
                                 f32("L0_attn_Wk"), f32("L0_attn_bk"),
                                 f32("L0_attn_Wv"), f32("L0_attn_bv"),
                                 f32("L0_attn_Wg"), f32("L0_attn_bg"),
                                 f32("L0_attn_Wo"), f32("L0_attn_bo")))
    fams["l0a"] = a

    a = _base_for("l1f")
    a.update(l1_ffn_host_arrays(f32("L1_ffn_weight"), f32("L1_ffn_disha")))
    fams["l1f"] = a

    a = _base_for("l1a")
    a.update(l1_attn_host_arrays(f32("L1_attn_w_in"), f32("L1_attn_b_in"),
                                 f32("L1_attn_A"), f32("L1_attn_Bm"),
                                 f32("L1_attn_Cm"), f32("L1_attn_Dv"),
                                 f32("L1_attn_w_out"), f32("L1_attn_b_out"),
                                 f32("L1_attn_g_norm")))
    fams["l1a"] = a

    a = _base_for("l2f")
    ltm = np.zeros((128, 8, 8), np.float32)
    for e in range(8):
        ltm[:, e, :e] = 1.0
    a["ltmask"] = np.ascontiguousarray(ltm.reshape(128, 64))
    a["ident128"] = np.eye(128, dtype=np.float32)
    a.update(l2_ffn_host_arrays(f32("L2_ffn_gate_w"), f32("L2_ffn_w1"),
                                f32("L2_ffn_b1"), f32("L2_ffn_ln1_g"),
                                f32("L2_ffn_ln1_b"), f32("L2_ffn_w2"),
                                f32("L2_ffn_b2"), f32("L2_ffn_ln_g"),
                                f32("L2_ffn_ln_b")))
    fams["l2f"] = a

    a = _base_for("l2a")
    a.update(l2_attn_host_arrays(
        f32("L2_attn_conv1_w"), f32("L2_attn_conv1_b"),
        f32("L2_attn_conv2_w"), f32("L2_attn_conv2_b"),
        f32("L2_attn_ln0_g"), f32("L2_attn_ln0_b"),
        f32("L2_attn_Wq"), f32("L2_attn_bq"),
        f32("L2_attn_Wk"), f32("L2_attn_bk"),
        f32("L2_attn_Wv"), f32("L2_attn_bv"),
        f32("L2_attn_lnA_g"), f32("L2_attn_lnA_b"),
        f32("L2_attn_W1"), f32("L2_attn_b1"),
        f32("L2_attn_W2"), f32("L2_attn_b2"),
        f32("L2_attn_ln2_g"), f32("L2_attn_ln2_b")))
    fams["l2a"] = a

    a = _base_for("l3f")
    a["l3_w1T"] = np.ascontiguousarray(f32("L3_ffn_W1").T)
    a["l3_w2T"] = np.ascontiguousarray(f32("L3_ffn_W2").T)
    a["l3_b1c"] = col_layout(f32("L3_ffn_b1"))
    a["l3_b2c"] = col_layout(f32("L3_ffn_b2"))
    fams["l3f"] = a

    a = _base_for("l3a")
    a["ident128"] = np.eye(128, dtype=np.float32)
    a.update(l3_attn_host_arrays(f32("L3_attn_w_down_q"),
                                 f32("L3_attn_w_down_kv"),
                                 f32("L3_attn_w_up_q"), f32("L3_attn_w_up_k"),
                                 f32("L3_attn_w_up_v"), f32("L3_attn_w_out"),
                                 f32("L3_attn_ln_scale"),
                                 f32("L3_attn_ln_bias")))
    fams["l3a"] = a
    return fams


def kernel(**inputs):
    x = np.ascontiguousarray(np.asarray(inputs["L0_attn_hidden_states"],
                                        np.float32))
    fams = _fam_arrays(inputs)
    progs = {k: build_program(_BODIES[k], fams[k]) for k in _BODIES}
    xf = [np.ascontiguousarray(fm_layout(x[b])) for b in range(8)]
    for famname in _SEQ:
        arrays = fams[famname]
        in_maps = [dict(arrays, x_fm=xf[b]) for b in range(8)]
        res = run_bass_kernel_spmd(progs[famname], in_maps,
                                   core_ids=list(range(8)))
        xf = [np.asarray(res.results[b]["x_out"], np.float32)
              for b in range(8)]
    return np.ascontiguousarray(
        np.stack([fm_unlayout(xf[b]) for b in range(8)], axis=0))
